# version 05
# baseline (speedup 1.0000x reference)
import numpy as np
from collections import deque
from contextlib import ExitStack

import concourse.bass as bass
import concourse.mybir as mybir
from concourse.bass_utils import run_bass_kernel_spmd

F32 = mybir.dt.float32
BF16 = mybir.dt.bfloat16
I32 = mybir.dt.int32
AF = mybir.ActivationFunctionType
ALU = mybir.AluOpType
AX = mybir.AxisListType

D = 1024
TC = 2048
SEQ = 4096
NCH = 8
H = 16
HD = 64
NE = 16
FE = 512
FS = 1024
EPS = 1e-5
ATTN_SCALE = HD ** -0.5

ARENA = 67968
R_XN = 0
R_SG = 16384
R_SU = 24576
R_SD = 32768
R_EXPA = 40960
R_EXPB = 16384
R_HE = 53248
R_HS = 57344
R_CWT = 61440
R_CWB = 63488
R_WR = 63744
R_K = 0
R_VT = 4096
R_Q = 8192
R_VS = 10240
R_A = 14592
R_QKVW = 18688
R_PROJW = 24832
R_AO = 26880
R_QLV = 30976
R_XNF = 35072
R_CROW = 63872
SLAB = 2080


class Eng:
    def __init__(self, name, sem):
        self.name = name
        self.sem = sem
        self.ops = []
        self.cnt = 0
        self.last_real = -1
        self.tokened = -1
        self.water = {}

    def op(self, fn):
        self.ops.append(fn)
        self.last_real = len(self.ops) - 1

    def mark_dma(self):
        self.tokened = self.last_real

    def tok(self, n=1):
        assert self.last_real > self.tokened, "tok with no fresh real op"
        self.cnt += n
        idx = self.last_real
        last = self.ops[idx]
        sem = self.sem

        def wrapped(e, _last=last, _n=n, _sem=sem):
            return _last(e).then_inc(_sem, _n)

        self.ops[idx] = wrapped
        self.tokened = idx
        return (sem, self.cnt)

    def mark(self):
        if self.last_real < 0:
            return None
        if self.last_real > self.tokened:
            return self.tok()
        return (self.sem, self.cnt)

    def wait(self, token):
        if token is None:
            return
        sem, val = token
        if self.water.get(id(sem), 0) >= val:
            return
        self.water[id(sem)] = val
        self.ops.append(lambda e, _s=sem, _v=val: e.wait_ge(_s, _v))

    def fence(self):
        self.wait(self.mark())


class Rot:
    def __init__(self, slices_with_toks):
        self.q = deque(slices_with_toks)

    def get(self, writer):
        ts, c0, free_tok = self.q.popleft()
        writer.wait(free_tok)
        return (ts, c0)

    def put(self, sl, free_tok):
        self.q.append((sl[0], sl[1], free_tok))

    def drain(self, eng):
        for (_, _, ft) in self.q:
            eng.wait(ft)


def slap(sl, p0=0, p1=128, w=512):
    ts, c0 = sl
    return ts[p0:p1, c0:c0 + w]


class Builder:
    def __init__(self, debug_stage=None):
        self.debug_stage = debug_stage
        nc = bass.Bass("TRN2", target_bir_lowering=False)
        self.nc = nc
        dt = nc.dram_tensor
        self.d_x = dt("xT", [D, TC], F32, kind="ExternalInput")
        self.d_wqkv = dt("wqkv", [D, 3 * D], F32, kind="ExternalInput")
        self.d_bqkv = dt("bqkv", [3 * D], F32, kind="ExternalInput")
        self.d_aq = dt("aq", [HD, 16], F32, kind="ExternalInput")
        self.d_bq = dt("bq", [16, HD], F32, kind="ExternalInput")
        self.d_av = dt("av", [HD, 16], F32, kind="ExternalInput")
        self.d_bv = dt("bv", [16, HD], F32, kind="ExternalInput")
        self.d_wproj = dt("wproj", [D, D], F32, kind="ExternalInput")
        self.d_bproj = dt("bproj", [D], F32, kind="ExternalInput")
        self.d_wr = dt("wr", [D, NE], F32, kind="ExternalInput")
        self.d_wg = dt("wg", [NE, D, FE], F32, kind="ExternalInput")
        self.d_wu = dt("wu", [NE, D, FE], F32, kind="ExternalInput")
        self.d_wd = dt("wd", [NE, FE, D], F32, kind="ExternalInput")
        self.d_sg = dt("sg", [D, FS], F32, kind="ExternalInput")
        self.d_su = dt("su", [D, FS], F32, kind="ExternalInput")
        self.d_sd = dt("sd", [FS, D], F32, kind="ExternalInput")
        self.d_po = dt("po", [128, 2], F32, kind="ExternalInput")
        self.d_o = dt("o", [D, TC], F32, kind="ExternalOutput")
        self.d_ccin = dt("ccin", [D, TC], BF16)
        self.d_ccout = dt("ccout", [2, D, TC], BF16)

        self.stack = ExitStack()
        es = self.stack.enter_context
        self.X = es(nc.sbuf_tensor("Xres", [128, NCH * TC], F32))
        self.AB = es(nc.sbuf_tensor("ABarena", [128, ARENA], BF16))
        self.SM = es(nc.sbuf_tensor("SMf32", [128, 1104], F32))
        self.BIAS = es(nc.sbuf_tensor("BIASV", [128, 32], F32))
        self.ONES = es(nc.sbuf_tensor("ONESC", [128, 1], BF16))
        self.ONESR = es(nc.sbuf_tensor("ONESR", [128, 128], BF16))
        self.IDN = es(nc.sbuf_tensor("IDN", [128, 128], BF16))
        self.AQ = es(nc.sbuf_tensor("AQs", [128, 16], BF16))
        self.BQ = es(nc.sbuf_tensor("BQs", [16, 64], BF16))
        self.AVm = es(nc.sbuf_tensor("AVs", [128, 16], BF16))
        self.BV = es(nc.sbuf_tensor("BVs", [16, 64], BF16))
        self.POS = es(nc.sbuf_tensor("POSs", [128, 2], F32))
        self.EPSC = es(nc.sbuf_tensor("EPSC", [1, 1], F32))
        self.PS12 = es(nc.psum_tensor("PS12", [128, 2048], F32))
        self.PS3 = es(nc.psum_tensor("PS3", [128, 1024], F32))
        self.PS4 = es(nc.psum_tensor("PS4", [128, 512], F32))
        self.PTF = es(nc.psum_tensor("PTF", [128, 512], F32))

        self.t = Eng("t", es(nc.semaphore("s_t")))
        self.v = Eng("v", es(nc.semaphore("s_v")))
        self.s = Eng("s", es(nc.semaphore("s_s")))
        self.g = Eng("g", es(nc.semaphore("s_g")))
        self.y = Eng("y", es(nc.semaphore("s_y")))
        self.cc = es(nc.semaphore("s_cc"))
        self.dma_groups = {}
        for gn in ("gin", "gwr", "gsh", "gsd", "gwA", "gwB", "gqA", "gqB",
                   "gpr", "yin", "ybc", "ycrA", "ycrB", "yout"):
            self.dma_groups[gn] = [es(nc.semaphore("s_" + gn)), 0]
        self.cc_cnt = 0
        self.engs = [self.t, self.v, self.s, self.g, self.y]
        self.dbg = {}

    def ab(self, off, ln, p0=0, p1=128):
        return self.AB[p0:p1, off:off + ln]

    def ab3(self, off, k, f):
        return self.AB[:, off:off + k * f].rearrange("p (k f) -> p k f", k=k)

    @property
    def LNROW(self):
        return self.AB[0:1, R_HS + 2048:R_HS + 4096].bitcast(F32)

    @property
    def FT(self):
        return self.AB[:, R_HS + 1024:R_HS + 2048].bitcast(F32)

    @property
    def CROW(self):
        return self.AB[0:1, R_CROW:R_CROW + 4096]

    def _dma(self, eng, grp, out, in_, **kw):
        g = self.dma_groups[grp]
        sem = g[0]
        eng.op(lambda e, _o=out, _i=in_, _k=kw, _s=sem:
               e.dma_start(out=_o, in_=_i, **_k).then_inc(_s, 16))
        eng.mark_dma()
        g[1] += 16
        return (sem, g[1])

    def gdma(self, grp, out, in_, **kw):
        return self._dma(self.g, grp, out, in_, **kw)

    def ydma(self, grp, out, in_, **kw):
        return self._dma(self.y, grp, out, in_, **kw)

    def mm(self, out, lhsT, rhs, start, stop, tp=None):
        self.t.op(lambda e, _o=out, _l=lhsT, _r=rhs, _a=start, _b=stop, _tp=tp:
                  e.matmul(_o, _l, _r, start=_a, stop=_b, tile_position=_tp))

    def barrier(self):
        toks = [e.mark() for e in self.engs]
        toks += [(g[0], g[1]) for g in self.dma_groups.values() if g[1] > 0]
        for e in self.engs:
            for tk in toks:
                e.wait(tk)

    def pool_slices(self, with_ps4=True, with_ps3=True, ps4_tok=None):
        sl = [(self.PS12, 0, None), (self.PS12, 512, None),
              (self.PS12, 1024, None), (self.PS12, 1536, None)]
        if with_ps3:
            sl += [(self.PS3, 0, None), (self.PS3, 512, None)]
        if with_ps4:
            sl.append((self.PS4, 0, ps4_tok))
        return sl

    def add_dbg(self, name, shape, dtype=F32):
        t = self.nc.dram_tensor("dbg_" + name, shape, dtype, kind="ExternalOutput")
        self.dbg[name] = t
        return t

    def init_consts(self):
        g = self.g
        g.op(lambda e: e.memset(self.ONES[:, :], 1.0))
        g.op(lambda e: e.memset(self.ONESR[:, :], 1.0))
        g.op(lambda e: e.memset(self.IDN[:, :], 0.0))
        g.fence()
        g.op(lambda e: e.affine_select(out=self.IDN[:, :], in_=self.IDN[:, :],
                                       compare_op=ALU.not_equal, fill=1.0, base=0,
                                       pattern=[[-1, 128]], channel_multiplier=1))
        g.op(lambda e: e.memset(self.EPSC[:, :], EPS))
        self.consts_tok = g.tok()
        self.gdma("gin", self.AQ[0:64, :], self.d_aq[:, :])
        self.gdma("gin", self.AQ[64:128, :], self.d_aq[:, :])
        self.gdma("gin", self.BQ[:, :], self.d_bq[:, :])
        self.gdma("gin", self.AVm[0:64, :], self.d_av[:, :])
        self.gdma("gin", self.AVm[64:128, :], self.d_av[:, :])
        self.gdma("gin", self.BV[:, :], self.d_bv[:, :])
        self.po_tok = self.gdma("gin", self.POS[:, :], self.d_po[:, :])
        self.lora_tok = self.po_tok
        brow = self.AB[0:1, R_CROW:R_CROW + 3072]
        brow2 = self.AB[0:1, R_CROW + 3072:R_CROW + 4096]
        self.gdma("gin", brow, self.d_bqkv.ap().rearrange("(o a) -> o a", o=1))
        br_tok = self.gdma("gin", brow2, self.d_bproj.ap().rearrange("(o a) -> o a", o=1))
        self.x_tok = self.ydma("yin", self.X[:, :].rearrange("p (k t) -> p k t", k=NCH),
                               self.d_x.ap().rearrange("(k p) t -> p k t", p=128))
        t = self.t
        t.wait(br_tok)
        t.wait(self.consts_tok)
        for oc in range(24):
            self.mm(self.PS4[:, oc:oc + 1],
                    self.AB[0:1, R_CROW + oc * 128:R_CROW + (oc + 1) * 128],
                    self.ONESR[0:1, 0:1], True, True)
        for oc in range(8):
            self.mm(self.PS4[:, 24 + oc:25 + oc],
                    self.AB[0:1, R_CROW + 3072 + oc * 128:R_CROW + 3072 + (oc + 1) * 128],
                    self.ONESR[0:1, 0:1], True, True)
        bmm_tok = t.tok()
        v = self.v
        v.wait(bmm_tok)
        v.op(lambda e: e.tensor_copy(self.BIAS[:, 0:32], self.PS4[:, 0:32]))
        self.bias_tok = v.tok()
        self.biasp_tok = self.bias_tok
        self.bias_ps4_free = self.bias_tok

    def layernorm(self, x_tok):
        t, v, s = self.t, self.v, self.s
        v.wait(x_tok)
        for c in range(NCH):
            v.op(lambda e, _c=c: e.tensor_copy(self.ab(R_XN + _c * TC, TC),
                                               self.X[:, _c * TC:(_c + 1) * TC]))
        xb_tok = v.tok()
        last_apply = None
        for tb in range(4):
            t0 = tb * 512
            s.wait(xb_tok)
            if last_apply is not None:
                s.wait(last_apply)
            for c in range(NCH):
                s.op(lambda e, _c=c, _t0=t0: e.activation(
                    self.ab(R_HE + _c * 512, 512),
                    self.ab(R_XN + _c * TC + _t0, 512), AF.Square))
            x2_tok = s.tok()
            t.wait(xb_tok)
            t.wait(x2_tok)
            t.wait(self.consts_tok)
            t.wait(self.bias_tok)
            if last_apply is not None:
                t.wait(last_apply)
            for c in range(NCH):
                self.mm(self.PS4[0:1, 0:512], self.ONES[:, :],
                        self.ab(R_XN + c * TC + t0, 512), c == 0, c == NCH - 1)
            for c in range(NCH):
                self.mm(self.PS3[0:1, 0:512], self.ONES[:, :],
                        self.ab(R_HE + c * 512, 512), c == 0, c == NCH - 1)
            stats_tok = t.tok()
            v.wait(stats_tok)
            v.op(lambda e: e.tensor_scalar(self.LNROW[0:1, 0:512],
                                           self.PS4[0:1, 0:512],
                                           1.0 / D, None, ALU.mult))
            v.op(lambda e: e.tensor_scalar(self.LNROW[0:1, 512:1024],
                                           self.PS3[0:1, 0:512],
                                           1.0 / D, None, ALU.mult))
            v.fence()
            v.op(lambda e: e.tensor_mul(self.FT[0:1, 0:512],
                                        self.LNROW[0:1, 0:512],
                                        self.LNROW[0:1, 0:512]))
            v.fence()
            v.op(lambda e: e.tensor_sub(self.LNROW[0:1, 512:1024],
                                        self.LNROW[0:1, 512:1024],
                                        self.FT[0:1, 0:512]))
            var_tok = v.tok()
            s.wait(var_tok)
            s.wait(self.consts_tok)
            s.op(lambda e: e.activation(self.FT[0:1, 0:512],
                                        self.LNROW[0:1, 512:1024], AF.Sqrt,
                                        bias=self.EPSC[0:1, 0:1]))
            sd_tok = s.tok()
            v.wait(sd_tok)
            v.op(lambda e: e.reciprocal(self.LNROW[0:1, 512:1024],
                                        self.FT[0:1, 0:512]))
            v.fence()
            v.op(lambda e: e.tensor_copy(self.AB[0:1, R_HS:R_HS + 512],
                                         self.LNROW[0:1, 0:512]))
            v.op(lambda e: e.tensor_copy(self.AB[0:1, R_HS + 512:R_HS + 1024],
                                         self.LNROW[0:1, 512:1024]))
            brow_tok = v.tok()
            t.wait(brow_tok)
            self.mm(self.PS12[:, 0:512], self.ONESR[0:1, :],
                    self.AB[0:1, R_HS:R_HS + 512], True, True)
            self.mm(self.PS12[:, 512:1024], self.ONESR[0:1, :],
                    self.AB[0:1, R_HS + 512:R_HS + 1024], True, True)
            rep_tok = t.tok()
            v.wait(rep_tok)
            ft8 = self.AB[:, R_HE:R_HE + 4096].bitcast(F32)
            for half in range(2):
                cs = range(half * 4, half * 4 + 4)
                for c in cs:
                    v.op(lambda e, _c=c, _t0=t0, _sl=(c % 4) * 512: e.tensor_tensor(
                        ft8[:, _sl:_sl + 512],
                        self.X[:, _c * TC + _t0:_c * TC + _t0 + 512],
                        self.PS12[:, 0:512], ALU.subtract))
                v.fence()
                for c in cs:
                    v.op(lambda e, _c=c, _t0=t0, _sl=(c % 4) * 512: e.tensor_tensor(
                        self.ab(R_XN + _c * TC + _t0, 512), ft8[:, _sl:_sl + 512],
                        self.PS12[:, 512:1024], ALU.mult))
                if half == 0:
                    v.fence()
            last_apply = v.tok()
        return last_apply

    def router(self, xn_tok, level=99):
        t, v, s = self.t, self.v, self.s
        wr_tok = self.gdma("gwr", self.ab3(R_WR, NCH, NE),
                           self.d_wr.ap().rearrange("(k p) e -> p k e", p=128))
        t.wait(xn_tok)
        t.wait(wr_tok)
        wr = self.ab3(R_WR, NCH, NE)
        for tc in range(16):
            for c in range(NCH):
                self.mm(self.PS4[:, tc * 16:(tc + 1) * 16],
                        self.ab(R_XN + c * TC + tc * 128, 128),
                        wr[:, c, :], c == 0, c == NCH - 1)
        logits_tok = t.tok()
        P, PM, TMP, CW = 0, 256, 512, 768
        NM, SU_, RC, M1, M2 = 1024, 1040, 1056, 1072, 1088
        v.wait(logits_tok)
        v.op(lambda e: e.tensor_copy(self.SM[:, P:P + 256], self.PS4[:, 0:256]))
        ps4_free = v.tok()
        v.wait(ps4_free)
        if level < 2:
            return ps4_free, ps4_free

        def seg(base, tc):
            return self.SM[:, base + tc * 16:base + (tc + 1) * 16]

        def col(base, tc):
            return self.SM[:, base + tc:base + tc + 1]

        for tc in range(16):
            v.op(lambda e, _tc=tc: e.reduce_max(col(NM, _tc), seg(P, _tc),
                                                axis=AX.X, negate=True))
        v.fence()
        for tc in range(16):
            v.op(lambda e, _tc=tc: e.tensor_scalar(seg(P, _tc), seg(P, _tc),
                                                   col(NM, _tc), None, ALU.add))
        sh_tok = v.tok()
        s.wait(sh_tok)
        s.op(lambda e: e.activation(self.SM[:, P:P + 256], self.SM[:, P:P + 256],
                                    AF.Exp))
        exp_tok = s.tok()
        if level < 3:
            return exp_tok, ps4_free
        v.wait(exp_tok)
        for tc in range(16):
            v.op(lambda e, _tc=tc: e.reduce_sum(col(SU_, _tc), seg(P, _tc), axis=AX.X))
        v.fence()
        v.op(lambda e: e.reciprocal(self.SM[:, RC:RC + 16], self.SM[:, SU_:SU_ + 16]))
        v.fence()
        for tc in range(16):
            v.op(lambda e, _tc=tc: e.tensor_scalar(seg(P, _tc), seg(P, _tc),
                                                   col(RC, _tc), None, ALU.mult))
        v.fence()
        for tc in range(16):
            v.op(lambda e, _tc=tc: e.reduce_max(col(M1, _tc), seg(P, _tc), axis=AX.X))
        v.fence()
        for tc in range(16):
            v.op(lambda e, _tc=tc: e.tensor_scalar(seg(PM, _tc), seg(P, _tc),
                                                   col(M1, _tc), None, ALU.is_ge))
        v.fence()
        for tc in range(16):
            v.op(lambda e, _tc=tc: e.tensor_scalar(seg(TMP, _tc), seg(PM, _tc),
                                                   -1.0, 1.0, ALU.mult, ALU.add))
        v.fence()
        for tc in range(16):
            v.op(lambda e, _tc=tc: e.tensor_mul(seg(TMP, _tc), seg(TMP, _tc),
                                                seg(P, _tc)))
        v.fence()
        for tc in range(16):
            v.op(lambda e, _tc=tc: e.reduce_max(col(M2, _tc), seg(TMP, _tc), axis=AX.X))
        v.fence()
        for tc in range(16):
            v.op(lambda e, _tc=tc: e.tensor_scalar(seg(TMP, _tc), seg(TMP, _tc),
                                                   col(M2, _tc), None, ALU.is_ge))
        v.fence()
        for tc in range(16):
            v.op(lambda e, _tc=tc: e.tensor_add(seg(TMP, _tc), seg(TMP, _tc),
                                                seg(PM, _tc)))
        v.fence()
        for tc in range(16):
            v.op(lambda e, _tc=tc: e.scalar_tensor_tensor(
                seg(CW, _tc), seg(TMP, _tc), 0.5, seg(P, _tc), ALU.mult, ALU.mult))
        v.fence()
        v.op(lambda e: e.tensor_copy(self.ab(R_CWB, 256), self.SM[:, CW:CW + 256]))
        cwb_tok = v.tok()
        if level < 4:
            return cwb_tok, ps4_free
        t.wait(cwb_tok)
        pt_free = None
        cwt_tok = None
        for tc in range(16):
            t.wait(pt_free)
            t.op(lambda e, _tc=tc: e.transpose(
                self.PTF[0:16, 0:64].bitcast(BF16),
                self.ab(R_CWB + _tc * 16, 16), self.IDN[:, :]))
            tt = t.tok()
            v.wait(tt)
            v.op(lambda e, _tc=tc: e.tensor_copy(
                self.ab(R_CWT + _tc * 128, 128, 0, 16),
                self.PTF[0:16, 0:64].bitcast(BF16)))
            pt_free = v.tok()
            cwt_tok = pt_free
        return cwt_tok, ps4_free

    def ffn_call(self, xn_tok, n_experts=NE, with_shared=True):
        t, v, s, g, y = self.t, self.v, self.s, self.g, self.y
        self.gdma("gsh", self.ab3(R_SG, NCH, FS),
                  self.d_sg.ap().rearrange("(k p) f -> p k f", p=128))
        sg_tok = self.gdma("gsh", self.ab3(R_SU, NCH, FS),
                           self.d_su.ap().rearrange("(k p) f -> p k f", p=128))
        sd_tok = self.gdma("gsd", self.ab3(R_SD, NCH, D),
                           self.d_sd.ap().rearrange("(k p) f -> p k f", p=128))
        cwt_tok, ps4_free0 = self.router(xn_tok)

        pool = Rot(self.pool_slices(with_ps4=False, with_ps3=True))
        t.wait(xn_tok)
        t.wait(sg_tok)
        sgv = self.ab3(R_SG, NCH, FS)
        suv = self.ab3(R_SU, NCH, FS)
        sdv = self.ab3(R_SD, NCH, D)
        last_gu_tok = None

        hs_slots = [R_HS, R_HE]
        slot_free = [None, None]
        pend = None

        def emit_shared_d(p):
            tb_, slot_, h_tok_ = p
            t0_ = tb_ * 512
            hv = self.ab3(hs_slots[slot_], NCH, 512)
            t.wait(sd_tok)
            t.wait(h_tok_)
            last_d = None
            for dc in range(NCH):
                sld = pool.get(t)
                for c in range(NCH):
                    self.mm(slap(sld), sdv[:, c, dc * 128:(dc + 1) * 128],
                            hv[:, c, :], c == 0, c == NCH - 1)
                dtok = t.tok()
                last_d = dtok
                v.wait(dtok)
                v.op(lambda e, _dc=dc, _t0=t0_, _sl=sld: e.tensor_tensor(
                    self.X[:, _dc * TC + _t0:_dc * TC + _t0 + 512],
                    self.X[:, _dc * TC + _t0:_dc * TC + _t0 + 512],
                    slap(_sl), ALU.add))
                pool.put(sld, v.tok())
            slot_free[slot_] = last_d

        for tb in range(4 if with_shared else 0):
            t0 = tb * 512
            slot = tb % 2
            base = hs_slots[slot]
            t.wait(slot_free[slot])
            h_tok = None
            for fc in range(NCH):
                slg = pool.get(t)
                for c in range(NCH):
                    self.mm(slap(slg), sgv[:, c, fc * 128:(fc + 1) * 128],
                            self.ab(R_XN + c * TC + t0, 512), c == 0, c == NCH - 1)
                gtok = t.tok()
                slu = pool.get(t)
                for c in range(NCH):
                    self.mm(slap(slu), suv[:, c, fc * 128:(fc + 1) * 128],
                            self.ab(R_XN + c * TC + t0, 512), c == 0, c == NCH - 1)
                utok = t.tok()
                last_gu_tok = utok
                s.wait(gtok)
                s.op(lambda e, _fc=fc, _sl=slg, _b=base: e.activation(
                    self.ab(_b + _fc * 512, 512), slap(_sl), AF.Silu))
                stok = s.tok()
                pool.put(slg, stok)
                v.wait(stok)
                v.wait(utok)
                v.op(lambda e, _fc=fc, _sl=slu, _b=base: e.scalar_tensor_tensor(
                    self.ab(_b + _fc * 512, 512), self.ab(_b + _fc * 512, 512),
                    0.5, slap(_sl), ALU.mult, ALU.mult))
                h_tok = v.tok()
                pool.put(slu, h_tok)
            if pend is not None:
                emit_shared_d(pend)
            pend = (tb, slot, h_tok)
        if pend is not None:
            emit_shared_d(pend)

        he_slots = [R_HE, R_HE + 2048, R_HS, R_HS + 2048]
        he_free = [None, None, None, None]
        crep_ps = [self.PS4, self.PTF]
        crep_free = [ps4_free0, cwt_tok]
        exp_w_free = [None, None]
        w_toks = [None, None]
        crow_tok = [None, None]
        crow_free = [None, None]
        regions = [R_EXPA, R_EXPB]
        epend = None

        def load_expert(e_i):
            par = e_i % 2
            reg = regions[par]
            grp = "gwA" if par == 0 else "gwB"
            g.wait(exp_w_free[par])
            if e_i == 1:
                g.wait(last_gu_tok)
            self.gdma(grp, self.ab3(reg, NCH, FE),
                      self.d_wg[e_i].rearrange("(k p) f -> p k f", p=128))
            self.gdma(grp, self.ab3(reg + 4096, NCH, FE),
                      self.d_wu[e_i].rearrange("(k p) f -> p k f", p=128))
            w_toks[par] = self.gdma(grp, self.ab3(reg + 8192, 4, D),
                                    self.d_wd[e_i].rearrange("(k p) f -> p k f", p=128))
            y.wait(cwt_tok)
            y.wait(crow_free[par])
            crow_tok[par] = self.ydma("ycrA" if par == 0 else "ycrB",
                                      self.CROW[0:1, par * TC:(par + 1) * TC],
                                      self.ab(R_CWT, TC, e_i, e_i + 1))

        def emit_exp_d(p):
            slot_, hc_tok_, wdv_, t0_, par_, is_last_tb_ = p
            hv = self.ab3(he_slots[slot_], 4, 512)
            t.wait(hc_tok_)
            last_d = None
            for dc in range(NCH):
                sld = pool.get(t)
                for c in range(4):
                    self.mm(slap(sld), wdv_[:, c, dc * 128:(dc + 1) * 128],
                            hv[:, c, :], c == 0, c == 3)
                dtok = t.tok()
                last_d = dtok
                v.wait(dtok)
                v.op(lambda e, _dc=dc, _t0=t0_, _sl=sld: e.tensor_tensor(
                    self.X[:, _dc * TC + _t0:_dc * TC + _t0 + 512],
                    self.X[:, _dc * TC + _t0:_dc * TC + _t0 + 512],
                    slap(_sl), ALU.add))
                pool.put(sld, v.tok())
            he_free[slot_] = last_d
            if is_last_tb_:
                exp_w_free[par_] = last_d

        if n_experts > 0:
            load_expert(0)
        for e_i in range(n_experts):
            par = e_i % 2
            reg = regions[par]
            wgv = self.ab3(reg, NCH, FE)
            wuv = self.ab3(reg + 4096, NCH, FE)
            wdv = self.ab3(reg + 8192, 4, D)
            t.wait(w_toks[par])
            t.wait(crow_tok[par])
            last_crep = None
            for tb in range(4):
                t0 = tb * 512
                gi = e_i * 4 + tb
                slot = gi % 4
                cs = gi % 2
                cps = crep_ps[cs]
                t.wait(crep_free[cs])
                self.mm(cps[:, 0:512], self.ONESR[0:1, :],
                        self.CROW[0:1, par * TC + t0:par * TC + t0 + 512], True, True)
                crep_tok = t.tok()
                last_crep = crep_tok
                base = he_slots[slot]
                t.wait(he_free[slot])
                for fc in range(4):
                    slg = pool.get(t)
                    for c in range(NCH):
                        self.mm(slap(slg), wgv[:, c, fc * 128:(fc + 1) * 128],
                                self.ab(R_XN + c * TC + t0, 512), c == 0, c == NCH - 1)
                    gtok = t.tok()
                    slu = pool.get(t)
                    for c in range(NCH):
                        self.mm(slap(slu), wuv[:, c, fc * 128:(fc + 1) * 128],
                                self.ab(R_XN + c * TC + t0, 512), c == 0, c == NCH - 1)
                    utok = t.tok()
                    s.wait(gtok)
                    s.op(lambda e, _fc=fc, _sl=slg, _b=base: e.activation(
                        self.ab(_b + _fc * 512, 512), slap(_sl), AF.Silu))
                    stok = s.tok()
                    pool.put(slg, stok)
                    v.wait(stok)
                    v.wait(utok)
                    v.op(lambda e, _fc=fc, _sl=slu, _b=base: e.tensor_tensor(
                        self.ab(_b + _fc * 512, 512), self.ab(_b + _fc * 512, 512),
                        slap(_sl), ALU.mult))
                    pool.put(slu, v.tok())
                v.wait(crep_tok)
                v.fence()
                for fc in range(4):
                    v.op(lambda e, _fc=fc, _b=base, _c=cps: e.tensor_tensor(
                        self.ab(_b + _fc * 512, 512), self.ab(_b + _fc * 512, 512),
                        _c[:, 0:512], ALU.mult))
                hc_tok = v.tok()
                crep_free[cs] = hc_tok
                epend_new = (slot, hc_tok, wdv, t0, par, tb == 3)
                if epend is not None:
                    emit_exp_d(epend)
                epend = epend_new
                if tb == 0 and e_i + 1 < n_experts:
                    load_expert(e_i + 1)
            crow_free[par] = last_crep
        if epend is not None:
            emit_exp_d(epend)
        pool.drain(t)
        return self.v.mark()

    def gather(self, xn_tok, t_done):
        g, y, v = self.g, self.y, self.v
        y.wait(xn_tok)
        bounce_tok = self.ydma("ybc", self.d_ccin.ap().rearrange("(k p) t -> p k t", p=128),
                               self.ab3(R_XN, NCH, TC))
        g.wait(bounce_tok)
        self.cc_cnt += 1
        cc = self.cc
        n = self.cc_cnt
        g.op(lambda e: e.collective_compute(
            "AllGather", ALU.bypass,
            replica_groups=[[0, 1], [2, 3], [4, 5], [6, 7]],
            ins=[self.d_ccin.ap().opt()],
            outs=[self.d_ccout.ap().opt()]).then_inc(cc, 1))
        g.mark_dma()
        cc_tok = (cc, n)
        v.wait(xn_tok)
        for c in range(NCH):
            v.op(lambda e, _c=c: e.tensor_copy(
                self.ab(R_XNF + _c * SEQ, TC), self.ab(R_XN + _c * TC, TC)))
        own_tok = v.tok()
        g.wait(t_done)
        g.wait(cc_tok)
        y.wait(cc_tok)
        xnf = self.ab3(R_XNF, NCH, SEQ)
        scr = self.ab3(R_QKVW, NCH, TC)
        tok_a = self.gdma("gpr", xnf[:, :, TC:SEQ],
                          self.d_ccout[0].rearrange("(k p) t -> p k t", p=128))
        tok_b = self.ydma("ybc", scr,
                          self.d_ccout[1].rearrange("(k p) t -> p k t", p=128))
        v.wait(tok_a)
        v.wait(tok_b)
        v.wait(self.po_tok)
        for c in range(NCH):
            v.op(lambda e, _c=c: e.tensor_scalar(
                self.ab(R_QKVW + _c * TC, TC), self.ab(R_QKVW + _c * TC, TC),
                self.POS[:, 0:1], None, ALU.mult))
        v.fence()
        for c in range(NCH):
            v.op(lambda e, _c=c: e.scalar_tensor_tensor(
                self.ab(R_XNF + _c * SEQ + TC, TC),
                self.ab(R_XNF + _c * SEQ + TC, TC),
                self.POS[:, 1:2],
                self.ab(R_QKVW + _c * TC, TC),
                ALU.mult, ALU.add))
        pair_tok = v.tok()
        return own_tok, pair_tok

    def attention(self, own_tok, pair_tok, guards=()):
        t, v, s, g, y = self.t, self.v, self.s, self.g, self.y
        for gk in guards:
            t.wait(gk)
        xnf = self.ab3(R_XNF, NCH, SEQ)
        pool = Rot(self.pool_slices(with_ps4=True, with_ps3=True))
        qkvw_free = [None, None]
        projw_free = [None, None]
        qkv_toks = [None, None]
        proj_toks = [None, None]
        ao_free = [None, None]
        kbuf_free = None
        vslab_free = None
        q_free = None
        vt_free = None
        a_free = [None, None, None, None]
        scores_ps_free = [None, None]
        av_ps_free = None
        qlv_free = None

        def load_pair_w(p):
            par = p % 2
            grp = "gqA" if par == 0 else "gqB"
            if p < 2:
                g.wait(pair_tok)
            g.wait(qkvw_free[par])
            g.wait(projw_free[par])
            base = R_QKVW + par * 3072
            w3 = self.ab3(base, NCH, 384)
            for j, c0 in enumerate((128 * p, D + 128 * p, 2 * D + 128 * p)):
                self.gdma(grp, w3[:, :, j * 128:(j + 1) * 128],
                          self.d_wqkv[:, c0:c0 + 128].rearrange(
                              "(k p) c -> p k c", p=128))
            tokw = self.gdma(grp, self.ab(R_PROJW + par * D, D),
                             self.d_wproj[128 * p:128 * (p + 1), :])
            qkv_toks[par] = tokw
            proj_toks[par] = tokw

        g.wait(own_tok)
        g.wait(pair_tok)
        g.op(lambda e: e.memset(self.ab(R_VS, 2 * SLAB), 1.0))
        vs_init = g.tok()
        vslab_free = vs_init
        load_pair_w(0)
        for p in range(8):
            par = p % 2
            if p + 1 < 8:
                load_pair_w(p + 1)
            w3 = self.ab3(R_QKVW + par * 3072, NCH, 384)
            t.wait(qkv_toks[par])
            s.wait(self.bias_tok)
            t.wait(kbuf_free)
            for tbl in range(8):
                t.wait(own_tok if tbl < 4 else pair_tok)
                sl = pool.get(t)
                for c in range(NCH):
                    self.mm(slap(sl), w3[:, c, 128:256],
                            xnf[:, c, tbl * 512:(tbl + 1) * 512], c == 0, c == NCH - 1)
                ktok = t.tok()
                s.wait(ktok)
                s.op(lambda e, _tbl=tbl, _sl=sl, _p=p: e.activation(
                    self.ab(R_K + _tbl * 512, 512), slap(_sl), AF.Identity,
                    bias=self.BIAS[:, 8 + _p:9 + _p]))
                pool.put(sl, s.tok())
            kb_tok = self.s.mark()
            t.wait(vt_free)
            for tbl in range(8):
                t.wait(own_tok if tbl < 4 else pair_tok)
                sl = pool.get(t)
                for c in range(NCH):
                    self.mm(slap(sl), w3[:, c, 256:384],
                            xnf[:, c, tbl * 512:(tbl + 1) * 512], c == 0, c == NCH - 1)
                vtok = t.tok()
                s.wait(vtok)
                s.op(lambda e, _tbl=tbl, _sl=sl, _p=p: e.activation(
                    self.ab(R_VT + _tbl * 512, 512), slap(_sl), AF.Identity,
                    bias=self.BIAS[:, 16 + _p:17 + _p]))
                pool.put(sl, s.tok())
            vt_tok = self.s.mark()
            t.wait(q_free)
            t.wait(own_tok)
            for tbl in range(4):
                sl = pool.get(t)
                for c in range(NCH):
                    self.mm(slap(sl), w3[:, c, 0:128],
                            xnf[:, c, tbl * 512:(tbl + 1) * 512], c == 0, c == NCH - 1)
                qtok = t.tok()
                s.wait(qtok)
                s.op(lambda e, _tbl=tbl, _sl=sl, _p=p: e.activation(
                    self.ab(R_Q + _tbl * 512, 512), slap(_sl), AF.Identity,
                    bias=self.BIAS[:, _p:_p + 1]))
                pool.put(sl, s.tok())
            q_tok = self.s.mark()
            t.wait(q_tok)
            t.wait(self.lora_tok)
            t.wait(qlv_free)
            for hh in range(2):
                b0 = hh * 64
                for tbl in range(4):
                    sl = pool.get(t)
                    self.mm(slap(sl, 0, 16), self.AQ[b0:b0 + 64, :],
                            self.ab(R_Q + tbl * 512, 512, b0, b0 + 64), True, True,
                            tp=(b0, 0))
                    qlt = t.tok()
                    s.wait(qlt)
                    s.op(lambda e, _tbl=tbl, _sl=sl, _hh=hh: e.activation(
                        self.ab(R_QLV + _hh * TC + _tbl * 512, 512, 0, 16),
                        slap(_sl, 0, 16), AF.Identity, scale=2.0))
                    pool.put(sl, s.tok())
            ql_tok = self.s.mark()
            t.wait(ql_tok)
            for hh in range(2):
                b0 = hh * 64
                for tbl in range(4):
                    sl = pool.get(t)
                    self.mm(slap(sl, b0, b0 + 64), self.BQ[:, :],
                            self.ab(R_QLV + hh * TC + tbl * 512, 512, 0, 16),
                            True, True, tp=(0, b0))
                    ct = t.tok()
                    v.wait(ct)
                    v.op(lambda e, _tbl=tbl, _sl=sl, _b0=b0: e.tensor_tensor(
                        self.ab(R_Q + _tbl * 512, 512, _b0, _b0 + 64),
                        self.ab(R_Q + _tbl * 512, 512, _b0, _b0 + 64),
                        slap(_sl, _b0, _b0 + 64), ALU.add))
                    pool.put(sl, v.tok())
            qfinal_tok = self.v.mark()
            t.wait(vt_tok)
            for hh in range(2):
                b0 = hh * 64
                for tbl in range(8):
                    sl = pool.get(t)
                    self.mm(slap(sl, 0, 16), self.AVm[b0:b0 + 64, :],
                            self.ab(R_VT + tbl * 512, 512, b0, b0 + 64), True, True,
                            tp=(b0, 0))
                    vlt = t.tok()
                    s.wait(vlt)
                    s.op(lambda e, _tbl=tbl, _sl=sl: e.activation(
                        self.ab(R_QLV + _tbl * 512, 512, 0, 16),
                        slap(_sl, 0, 16), AF.Identity, scale=2.0))
                    pool.put(sl, s.tok())
                vl_tok = self.s.mark()
                t.wait(vl_tok)
                for tbl in range(8):
                    sl = pool.get(t)
                    self.mm(slap(sl, b0, b0 + 64), self.BV[:, :],
                            self.ab(R_QLV + tbl * 512, 512, 0, 16),
                            True, True, tp=(0, b0))
                    ct = t.tok()
                    v.wait(ct)
                    v.op(lambda e, _tbl=tbl, _sl=sl, _b0=b0: e.tensor_tensor(
                        self.ab(R_VT + _tbl * 512, 512, _b0, _b0 + 64),
                        self.ab(R_VT + _tbl * 512, 512, _b0, _b0 + 64),
                        slap(_sl, _b0, _b0 + 64), ALU.add))
                    pool.put(sl, v.tok())
            vfinal_tok = self.v.mark()
            qlv_free = vfinal_tok
            pool.drain(t)
            t.wait(vfinal_tok)
            t.wait(vslab_free)
            v.wait(vslab_free)
            tsl = [self.PS12[:, 0:512], self.PS12[:, 512:1024]]
            pt_free = [None, None]
            for tc in range(32):
                sl_i = tc % 2
                t.wait(pt_free[sl_i])
                t.op(lambda e, _tc=tc, _s=tsl[sl_i]: e.transpose(
                    _s[:, 0:64].bitcast(BF16),
                    self.ab(R_VT + _tc * 128, 128), self.IDN[:, :]))
                trt = t.tok()
                v.wait(trt)
                v.op(lambda e, _tc=tc, _s=tsl[sl_i]: e.tensor_copy(
                    self.ab(R_VS + _tc * 65, 64),
                    _s[:, 0:32].bitcast(BF16)))
                v.op(lambda e, _tc=tc, _s=tsl[sl_i]: e.tensor_copy(
                    self.ab(R_VS + SLAB + _tc * 65, 64),
                    _s[:, 32:64].bitcast(BF16)))
                pt_free[sl_i] = v.tok()
            slab_tok = self.v.mark()
            vt_free = self.t.mark()
            t.wait(slab_tok)
            t.wait(qfinal_tok)
            ao_base = R_AO + par * TC
            v.wait(ao_free[par])
            avs = [(self.PS3, 0), (self.PS3, 512), (self.PS4, 0), (self.PTF, 0)]
            norm_tok = None
            sps_free = [None, None]
            a_free2 = [[None, None], [None, None]]
            for qp in range(2):
                q0 = qp * 1024
                for kc in range(32):
                    apar = kc % 2
                    t.wait(sps_free[0])
                    t.wait(a_free2[0][apar])
                    for hh in range(2):
                        b0 = hh * 64
                        for i in range(2):
                            self.mm(self.PS12[:, (hh * 2 + i) * 512:
                                              (hh * 2 + i) * 512 + 512],
                                    self.ab(R_K + kc * 128, 128, b0, b0 + 64),
                                    self.ab(R_Q + q0 + i * 512, 512, b0, b0 + 64),
                                    True, True, tp=(b0, 0))
                    sc_tok = t.tok()
                    s.wait(sc_tok)
                    s.op(lambda e, _ap=apar: e.activation(
                        self.ab(R_A + _ap * 2048, 2048), self.PS12[:, :],
                        AF.Exp, scale=ATTN_SCALE))
                    sps_free[0] = s.tok()
                    t.wait(sps_free[0])
                    if kc == 0:
                        t.wait(norm_tok)
                    slot = R_A + apar * 2048
                    for hh in range(2):
                        for i in range(2):
                            av = avs[hh * 2 + i]
                            self.mm(av[0][0:65, av[1]:av[1] + 512],
                                    self.ab(R_VS + hh * SLAB + kc * 65, 65),
                                    self.ab(slot + (hh * 2 + i) * 512, 512),
                                    kc == 0, kc == 31)
                    a_free2[0][apar] = t.tok()
                av_tok = self.t.mark()
                f32a = self.AB[:, R_A:R_A + 2048].bitcast(F32)
                f32b = self.AB[:, R_A + 2048:R_A + 4096].bitcast(F32)
                v.wait(av_tok)
                v.op(lambda e: e.tensor_copy(f32a[0:1, 0:512], self.PS3[64:65, 0:512]))
                v.op(lambda e: e.tensor_copy(f32a[0:1, 512:1024],
                                             self.PS3[64:65, 512:1024]))
                v.op(lambda e: e.tensor_copy(f32a[32:33, 0:512], self.PS4[64:65, 0:512]))
                v.op(lambda e: e.tensor_copy(f32a[32:33, 512:1024],
                                             self.PTF[64:65, 0:512]))
                v.fence()
                v.op(lambda e: e.reciprocal(f32b[0:1, 0:1024], f32a[0:1, 0:1024]))
                v.op(lambda e: e.reciprocal(f32b[32:33, 0:1024], f32a[32:33, 0:1024]))
                v.fence()
                rb = R_QLV + TC
                v.op(lambda e, _rb=rb: e.tensor_copy(self.AB[0:1, _rb:_rb + 1024],
                                                     f32b[0:1, 0:1024]))
                v.op(lambda e, _rb=rb: e.tensor_copy(self.AB[32:33, _rb:_rb + 1024],
                                                     f32b[32:33, 0:1024]))
                rc_tok = v.tok()
                scr = R_A + 4096 - 2048
                scr = R_QLV
                v.wait(av_tok)
                for j, av in enumerate(avs):
                    v.op(lambda e, _j=j, _av=av: e.tensor_copy(
                        self.ab(scr + _j * 512, 512, 0, 64),
                        _av[0][0:64, _av[1]:_av[1] + 512]))
                av_evac_tok = v.tok()
                t.wait(rc_tok)
                t.wait(av_evac_tok)
                self.mm(self.PS3[0:64, 0:512], self.ONESR[0:1, 0:64],
                        self.AB[0:1, rb:rb + 512], True, True, tp=(0, 0))
                self.mm(self.PS3[0:64, 512:1024], self.ONESR[0:1, 0:64],
                        self.AB[0:1, rb + 512:rb + 1024], True, True, tp=(0, 0))
                self.mm(self.PS4[64:128, 0:512], self.ONESR[32:33, 0:64],
                        self.AB[32:33, rb:rb + 512], True, True, tp=(32, 64))
                self.mm(self.PTF[64:128, 0:512], self.ONESR[32:33, 0:64],
                        self.AB[32:33, rb + 512:rb + 1024], True, True, tp=(32, 64))
                rep_tok = t.tok()
                norm_ps_free = rep_tok
                v.wait(rep_tok)
                for j in range(4):
                    hh, qbi = j // 2, j % 2
                    if hh == 0:
                        rep_ap = self.PS3[0:64, qbi * 512:qbi * 512 + 512]
                    elif qbi == 0:
                        rep_ap = self.PS4[64:128, 0:512]
                    else:
                        rep_ap = self.PTF[64:128, 0:512]
                    p0b = hh * 64
                    v.op(lambda e, _j=j, _qbi=qbi, _rep=rep_ap, _p0=p0b,
                         _q0=q0, _ao=ao_base: e.tensor_tensor(
                        self.ab(_ao + _q0 + _qbi * 512, 512, _p0, _p0 + 64),
                        self.ab(scr + _j * 512, 512, 0, 64),
                        _rep, ALU.mult))
                norm_tok = v.tok()
            ao_tok = self.v.mark()
            kbuf_free = self.t.mark()
            vslab_free = kbuf_free
            q_free = kbuf_free
            qkvw_free[par] = kbuf_free
            t.wait(proj_toks[par])
            t.wait(ao_tok)
            for dc in range(NCH):
                for tbl in range(4):
                    sl = pool.get(t)
                    self.mm(slap(sl), self.ab(R_PROJW + par * D + dc * 128, 128),
                            self.ab(ao_base + tbl * 512, 512), True, True)
                    ptok = t.tok()
                    v.wait(ptok)
                    v.op(lambda e, _dc=dc, _tbl=tbl, _sl=sl: e.tensor_tensor(
                        self.X[:, _dc * TC + _tbl * 512:_dc * TC + _tbl * 512 + 512],
                        self.X[:, _dc * TC + _tbl * 512:_dc * TC + _tbl * 512 + 512],
                        slap(_sl), ALU.add))
                    pool.put(sl, v.tok())
            projw_free[par] = self.t.mark()
            ao_free[par] = projw_free[par]
        xadd_tok = self.v.mark()
        s.wait(xadd_tok)
        s.wait(self.biasp_tok)
        for dc in range(NCH):
            s.op(lambda e, _dc=dc: e.add(self.X[:, _dc * TC:(_dc + 1) * TC],
                                         self.X[:, _dc * TC:(_dc + 1) * TC],
                                         self.BIAS[:, 24 + _dc:25 + _dc]))
        return self.s.mark()

    def dump_bf16(self, name, off, k, f):
        d = self.add_dbg(name, [k * 128, f], BF16)
        tk = self.ydma("yout", d.ap().rearrange("(k p) t -> p k t", p=128),
                       self.ab3(off, k, f))
        self.y.wait(tk)

    def dump_X(self, name):
        d = self.add_dbg(name, [D, TC], F32)
        tk = self.ydma("yout", d.ap().rearrange("(k p) t -> p k t", p=128),
                       self.X[:, :].rearrange("p (k t) -> p k t", k=NCH))
        self.y.wait(tk)

    def dump_sm(self, name, c0, c1):
        d = self.add_dbg(name, [128, c1 - c0], F32)
        tk = self.ydma("yout", d[:, :], self.SM[:, c0:c1])
        self.y.wait(tk)

    def build(self):
        ds = self.debug_stage
        self.init_consts()
        xn_tok = self.layernorm(self.x_tok)
        if ds == "ln1":
            self.barrier()
            self.dump_bf16("xn", R_XN, NCH, TC)
            return self.finish()
        if ds and ds.startswith("rl"):
            self.router(xn_tok, level=int(ds[2:]))
            self.barrier()
            self.dump_sm("probs", 0, 256)
            return self.finish()
        if ds == "router":
            self.router(xn_tok)
            self.barrier()
            self.dump_sm("probs", 0, 256)
            self.dump_sm("cw", 768, 1024)
            d = self.add_dbg("cwt", [16, TC], BF16)
            tk = self.ydma("yout", d[:, :], self.ab(R_CWT, TC, 0, 16))
            self.y.wait(tk)
            return self.finish()
        if ds and ds.startswith("ffn1"):
            ne = NE
            wsh = True
            if ds == "ffn1s":
                ne = 0
            elif ds.startswith("ffn1e"):
                ne = int(ds[5:])
                wsh = False
            self.ffn_call(xn_tok, n_experts=ne, with_shared=wsh)
            self.barrier()
            self.dump_X("x1")
            return self.finish()
        self.ffn_call(xn_tok)
        self.barrier()
        xn2_tok = self.layernorm(self.v.mark())
        own_tok, pair_tok = self.gather(xn2_tok, self.t.mark())
        if ds == "gather":
            self.barrier()
            self.dump_bf16("xnf", R_XNF, NCH, SEQ)
            return self.finish()
        self.attention(own_tok, pair_tok,
                       guards=(self.v.mark(), self.s.mark()))
        self.barrier()
        if ds == "attn":
            self.dump_X("x2")
            return self.finish()
        xn3_tok = self.layernorm(self.s.mark())
        self.ffn_call(xn3_tok)
        self.barrier()
        out_tok = self.ydma("yout", self.d_o.ap().rearrange("(k p) t -> p k t", p=128),
                            self.X[:, :].rearrange("p (k t) -> p k t", k=NCH))
        self.y.wait(out_tok)
        return self.finish()

    def finish(self):
        nc = self.nc
        with nc.Block() as block:
            @block.tensor
            def _(e):
                for f in self.t.ops:
                    f(e)

            @block.vector
            def _(e):
                for f in self.v.ops:
                    f(e)

            @block.scalar
            def _(e):
                for f in self.s.ops:
                    f(e)

            @block.gpsimd
            def _(e):
                for f in self.g.ops:
                    f(e)

            @block.sync
            def _(e):
                for f in self.y.ops:
                    f(e)
        self.stack.close()
        return nc


def build_nc(debug_stage=None):
    return Builder(debug_stage).build()


def make_in_maps(inputs):
    x = np.asarray(inputs["L0_attn_x"], dtype=np.float32).reshape(16384, D)
    cf = np.ascontiguousarray
    common = {
        "wqkv": cf(np.asarray(inputs["L0_attn_w_qkv"], np.float32)),
        "bqkv": cf(np.asarray(inputs["L0_attn_b_qkv"], np.float32)),
        "aq": cf(np.asarray(inputs["L0_attn_a_q"], np.float32)),
        "bq": cf(np.asarray(inputs["L0_attn_b_q"], np.float32)),
        "av": cf(np.asarray(inputs["L0_attn_a_v"], np.float32)),
        "bv": cf(np.asarray(inputs["L0_attn_b_v"], np.float32)),
        "wproj": cf(np.asarray(inputs["L0_attn_w_proj"], np.float32)),
        "bproj": cf(np.asarray(inputs["L0_attn_b_proj"], np.float32)),
        "wr": cf(np.asarray(inputs["L0_ffn_Wr"], np.float32)),
        "wg": cf(np.asarray(inputs["L0_ffn_Wg"], np.float32)),
        "wu": cf(np.asarray(inputs["L0_ffn_Wu"], np.float32)),
        "wd": cf(np.asarray(inputs["L0_ffn_Wd"], np.float32)),
        "sg": cf(np.asarray(inputs["L0_ffn_Sg"], np.float32)),
        "su": cf(np.asarray(inputs["L0_ffn_Su"], np.float32)),
        "sd": cf(np.asarray(inputs["L0_ffn_Sd"], np.float32)),
    }
    maps = []
    for c in range(8):
        m = dict(common)
        m["xT"] = np.ascontiguousarray(x[c * TC:(c + 1) * TC].T)
        mval = 1.0 - (c % 2)
        m["po"] = np.ascontiguousarray(
            np.tile(np.array([[mval, 1.0 - mval]], np.float32), (128, 1)))
        maps.append(m)
    return maps


_NC = None


def kernel(**inputs):
    global _NC
    if _NC is None:
        _NC = build_nc()
    maps = make_in_maps(inputs)
    res = run_bass_kernel_spmd(_NC, maps, core_ids=list(range(8)))
    out = np.empty((16384, D), np.float32)
    for c in range(8):
        out[c * TC:(c + 1) * TC] = res.results[c]["o"].T
    return out.reshape(4, SEQ, D)


# version 06
# speedup vs baseline: 1.0824x; 1.0824x over previous
import numpy as np
from collections import deque
from contextlib import ExitStack

import concourse.bass as bass
import concourse.mybir as mybir
from concourse.bass_utils import run_bass_kernel_spmd

F32 = mybir.dt.float32
BF16 = mybir.dt.bfloat16
I32 = mybir.dt.int32
AF = mybir.ActivationFunctionType
ALU = mybir.AluOpType
AX = mybir.AxisListType

D = 1024
TC = 2048
SEQ = 4096
NCH = 8
H = 16
HD = 64
NE = 16
FE = 512
FS = 1024
EPS = 1e-5
ATTN_SCALE = HD ** -0.5

ARENA = 67968
R_XN = 0
R_SG = 16384
R_SU = 24576
R_SD = 32768
R_EXPA = 40960
R_EXPB = 16384
R_HE = 53248
R_HS = 57344
R_CWT = 61440
R_CWB = 63488
R_WR = 63744
R_K = 0
R_VT = 4096
R_Q = 8192
R_VS = 10240
R_A = 14592
R_QKVW = 18688
R_PROJW = 24832
R_AO = 26880
R_QLV = 30976
R_XNF = 35072
R_CROW = 63872
SLAB = 2080


class Eng:
    def __init__(self, name, sem):
        self.name = name
        self.sem = sem
        self.ops = []
        self.cnt = 0
        self.last_real = -1
        self.tokened = -1
        self.water = {}

    def op(self, fn):
        self.ops.append(fn)
        self.last_real = len(self.ops) - 1

    def mark_dma(self):
        self.tokened = self.last_real

    def tok(self, n=1):
        assert self.last_real > self.tokened, "tok with no fresh real op"
        self.cnt += n
        idx = self.last_real
        last = self.ops[idx]
        sem = self.sem

        def wrapped(e, _last=last, _n=n, _sem=sem):
            return _last(e).then_inc(_sem, _n)

        self.ops[idx] = wrapped
        self.tokened = idx
        return (sem, self.cnt)

    def mark(self):
        if self.last_real < 0:
            return None
        if self.last_real > self.tokened:
            return self.tok()
        return (self.sem, self.cnt)

    def wait(self, token):
        if token is None:
            return
        sem, val = token
        if self.water.get(id(sem), 0) >= val:
            return
        self.water[id(sem)] = val
        self.ops.append(lambda e, _s=sem, _v=val: e.wait_ge(_s, _v))

    def fence(self):
        self.wait(self.mark())


class Rot:
    def __init__(self, slices_with_toks):
        self.q = deque(slices_with_toks)

    def get(self, writer):
        ts, c0, free_tok = self.q.popleft()
        writer.wait(free_tok)
        return (ts, c0)

    def put(self, sl, free_tok):
        self.q.append((sl[0], sl[1], free_tok))

    def drain(self, eng):
        for (_, _, ft) in self.q:
            eng.wait(ft)


def slap(sl, p0=0, p1=128, w=512):
    ts, c0 = sl
    return ts[p0:p1, c0:c0 + w]


class Builder:
    def __init__(self, debug_stage=None):
        self.debug_stage = debug_stage
        nc = bass.Bass("TRN2", target_bir_lowering=False)
        self.nc = nc
        dt = nc.dram_tensor
        self.d_x = dt("xT", [D, TC], F32, kind="ExternalInput")
        self.d_wqkv = dt("wqkv", [D, 3 * D], F32, kind="ExternalInput")
        self.d_bqkv = dt("bqkv", [3 * D], F32, kind="ExternalInput")
        self.d_aq = dt("aq", [HD, 16], F32, kind="ExternalInput")
        self.d_bq = dt("bq", [16, HD], F32, kind="ExternalInput")
        self.d_av = dt("av", [HD, 16], F32, kind="ExternalInput")
        self.d_bv = dt("bv", [16, HD], F32, kind="ExternalInput")
        self.d_wproj = dt("wproj", [D, D], F32, kind="ExternalInput")
        self.d_bproj = dt("bproj", [D], F32, kind="ExternalInput")
        self.d_wr = dt("wr", [D, NE], F32, kind="ExternalInput")
        self.d_wg = dt("wg", [NE, D, FE], F32, kind="ExternalInput")
        self.d_wu = dt("wu", [NE, D, FE], F32, kind="ExternalInput")
        self.d_wd = dt("wd", [NE, FE, D], F32, kind="ExternalInput")
        self.d_sg = dt("sg", [D, FS], F32, kind="ExternalInput")
        self.d_su = dt("su", [D, FS], F32, kind="ExternalInput")
        self.d_sd = dt("sd", [FS, D], F32, kind="ExternalInput")
        self.d_po = dt("po", [128, 2], F32, kind="ExternalInput")
        self.d_o = dt("o", [D, TC], F32, kind="ExternalOutput")
        self.d_ccin = dt("ccin", [D, TC], BF16)
        self.d_ccout = dt("ccout", [2, D, TC], BF16)

        self.stack = ExitStack()
        es = self.stack.enter_context
        self.X = es(nc.sbuf_tensor("Xres", [128, NCH * TC], F32))
        self.AB = es(nc.sbuf_tensor("ABarena", [128, ARENA], BF16))
        self.SM = es(nc.sbuf_tensor("SMf32", [128, 1104], F32))
        self.BIAS = es(nc.sbuf_tensor("BIASV", [128, 32], F32))
        self.ONES = es(nc.sbuf_tensor("ONESC", [128, 1], BF16))
        self.ONESR = es(nc.sbuf_tensor("ONESR", [128, 128], BF16))
        self.IDN = es(nc.sbuf_tensor("IDN", [128, 128], BF16))
        self.AQ = es(nc.sbuf_tensor("AQs", [128, 16], BF16))
        self.BQ = es(nc.sbuf_tensor("BQs", [16, 64], BF16))
        self.AVm = es(nc.sbuf_tensor("AVs", [128, 16], BF16))
        self.BV = es(nc.sbuf_tensor("BVs", [16, 64], BF16))
        self.POS = es(nc.sbuf_tensor("POSs", [128, 2], F32))
        self.EPSC = es(nc.sbuf_tensor("EPSC", [1, 1], F32))
        self.PS12 = es(nc.psum_tensor("PS12", [128, 2048], F32))
        self.PS3 = es(nc.psum_tensor("PS3", [128, 1024], F32))
        self.PS4 = es(nc.psum_tensor("PS4", [128, 512], F32))
        self.PTF = es(nc.psum_tensor("PTF", [128, 512], F32))

        self.t = Eng("t", es(nc.semaphore("s_t")))
        self.v = Eng("v", es(nc.semaphore("s_v")))
        self.s = Eng("s", es(nc.semaphore("s_s")))
        self.g = Eng("g", es(nc.semaphore("s_g")))
        self.y = Eng("y", es(nc.semaphore("s_y")))
        self.cc = es(nc.semaphore("s_cc"))
        self.dma_groups = {}
        for gn in ("gin", "gwr", "gsh", "gsd", "gwA", "gwB", "gqA", "gqB",
                   "gpr", "yin", "ybc", "ycrA", "ycrB", "yout"):
            self.dma_groups[gn] = [es(nc.semaphore("s_" + gn)), 0]
        self.cc_cnt = 0
        self.engs = [self.t, self.v, self.s, self.g, self.y]
        self.dbg = {}

    def ab(self, off, ln, p0=0, p1=128):
        return self.AB[p0:p1, off:off + ln]

    def ab3(self, off, k, f):
        return self.AB[:, off:off + k * f].rearrange("p (k f) -> p k f", k=k)

    @property
    def LNROW(self):
        return self.AB[0:1, R_HS + 2048:R_HS + 4096].bitcast(F32)

    @property
    def FT(self):
        return self.AB[:, R_HS + 1024:R_HS + 2048].bitcast(F32)

    @property
    def CROW(self):
        return self.AB[0:1, R_CROW:R_CROW + 4096]

    def _dma(self, eng, grp, out, in_, **kw):
        g = self.dma_groups[grp]
        sem = g[0]
        eng.op(lambda e, _o=out, _i=in_, _k=kw, _s=sem:
               e.dma_start(out=_o, in_=_i, **_k).then_inc(_s, 16))
        eng.mark_dma()
        g[1] += 16
        return (sem, g[1])

    def gdma(self, grp, out, in_, **kw):
        return self._dma(self.g, grp, out, in_, **kw)

    def ydma(self, grp, out, in_, **kw):
        return self._dma(self.y, grp, out, in_, **kw)

    def mm(self, out, lhsT, rhs, start, stop, tp=None):
        self.t.op(lambda e, _o=out, _l=lhsT, _r=rhs, _a=start, _b=stop, _tp=tp:
                  e.matmul(_o, _l, _r, start=_a, stop=_b, tile_position=_tp))

    def barrier(self):
        toks = [e.mark() for e in self.engs]
        toks += [(g[0], g[1]) for g in self.dma_groups.values() if g[1] > 0]
        for e in self.engs:
            for tk in toks:
                e.wait(tk)

    def pool_slices(self, with_ps4=True, with_ps3=True, ps4_tok=None):
        sl = [(self.PS12, 0, None), (self.PS12, 512, None),
              (self.PS12, 1024, None), (self.PS12, 1536, None)]
        if with_ps3:
            sl += [(self.PS3, 0, None), (self.PS3, 512, None)]
        if with_ps4:
            sl.append((self.PS4, 0, ps4_tok))
        return sl

    def add_dbg(self, name, shape, dtype=F32):
        t = self.nc.dram_tensor("dbg_" + name, shape, dtype, kind="ExternalOutput")
        self.dbg[name] = t
        return t

    def init_consts(self):
        g = self.g
        g.op(lambda e: e.memset(self.ONES[:, :], 1.0))
        g.op(lambda e: e.memset(self.ONESR[:, :], 1.0))
        g.op(lambda e: e.memset(self.IDN[:, :], 0.0))
        g.fence()
        g.op(lambda e: e.affine_select(out=self.IDN[:, :], in_=self.IDN[:, :],
                                       compare_op=ALU.not_equal, fill=1.0, base=0,
                                       pattern=[[-1, 128]], channel_multiplier=1))
        g.op(lambda e: e.memset(self.EPSC[:, :], EPS))
        self.consts_tok = g.tok()
        self.gdma("gin", self.AQ[0:64, :], self.d_aq[:, :])
        self.gdma("gin", self.AQ[64:128, :], self.d_aq[:, :])
        self.gdma("gin", self.BQ[:, :], self.d_bq[:, :])
        self.gdma("gin", self.AVm[0:64, :], self.d_av[:, :])
        self.gdma("gin", self.AVm[64:128, :], self.d_av[:, :])
        self.gdma("gin", self.BV[:, :], self.d_bv[:, :])
        self.po_tok = self.gdma("gin", self.POS[:, :], self.d_po[:, :])
        self.lora_tok = self.po_tok
        brow = self.AB[0:1, R_CROW:R_CROW + 3072]
        brow2 = self.AB[0:1, R_CROW + 3072:R_CROW + 4096]
        self.gdma("gin", brow, self.d_bqkv.ap().rearrange("(o a) -> o a", o=1))
        br_tok = self.gdma("gin", brow2, self.d_bproj.ap().rearrange("(o a) -> o a", o=1))
        self.x_tok = self.ydma("yin", self.X[:, :].rearrange("p (k t) -> p k t", k=NCH),
                               self.d_x.ap().rearrange("(k p) t -> p k t", p=128))
        t = self.t
        t.wait(br_tok)
        t.wait(self.consts_tok)
        for oc in range(24):
            self.mm(self.PS4[:, oc:oc + 1],
                    self.AB[0:1, R_CROW + oc * 128:R_CROW + (oc + 1) * 128],
                    self.ONESR[0:1, 0:1], True, True)
        for oc in range(8):
            self.mm(self.PS4[:, 24 + oc:25 + oc],
                    self.AB[0:1, R_CROW + 3072 + oc * 128:R_CROW + 3072 + (oc + 1) * 128],
                    self.ONESR[0:1, 0:1], True, True)
        bmm_tok = t.tok()
        v = self.v
        v.wait(bmm_tok)
        v.op(lambda e: e.tensor_copy(self.BIAS[:, 0:32], self.PS4[:, 0:32]))
        self.bias_tok = v.tok()
        self.biasp_tok = self.bias_tok
        self.bias_ps4_free = self.bias_tok

    def layernorm(self, x_tok):
        t, v, s = self.t, self.v, self.s
        v.wait(x_tok)
        for c in range(NCH):
            v.op(lambda e, _c=c: e.tensor_copy(self.ab(R_XN + _c * TC, TC),
                                               self.X[:, _c * TC:(_c + 1) * TC]))
        xb_tok = v.tok()
        last_apply = None
        for tb in range(4):
            t0 = tb * 512
            s.wait(xb_tok)
            if last_apply is not None:
                s.wait(last_apply)
            for c in range(NCH):
                s.op(lambda e, _c=c, _t0=t0: e.activation(
                    self.ab(R_HE + _c * 512, 512),
                    self.ab(R_XN + _c * TC + _t0, 512), AF.Square))
            x2_tok = s.tok()
            t.wait(xb_tok)
            t.wait(x2_tok)
            t.wait(self.consts_tok)
            t.wait(self.bias_tok)
            if last_apply is not None:
                t.wait(last_apply)
            for c in range(NCH):
                self.mm(self.PS4[0:1, 0:512], self.ONES[:, :],
                        self.ab(R_XN + c * TC + t0, 512), c == 0, c == NCH - 1)
            for c in range(NCH):
                self.mm(self.PS3[0:1, 0:512], self.ONES[:, :],
                        self.ab(R_HE + c * 512, 512), c == 0, c == NCH - 1)
            stats_tok = t.tok()
            v.wait(stats_tok)
            v.op(lambda e: e.tensor_scalar(self.LNROW[0:1, 0:512],
                                           self.PS4[0:1, 0:512],
                                           1.0 / D, None, ALU.mult))
            v.op(lambda e: e.tensor_scalar(self.LNROW[0:1, 512:1024],
                                           self.PS3[0:1, 0:512],
                                           1.0 / D, None, ALU.mult))
            v.fence()
            v.op(lambda e: e.tensor_mul(self.FT[0:1, 0:512],
                                        self.LNROW[0:1, 0:512],
                                        self.LNROW[0:1, 0:512]))
            v.fence()
            v.op(lambda e: e.tensor_sub(self.LNROW[0:1, 512:1024],
                                        self.LNROW[0:1, 512:1024],
                                        self.FT[0:1, 0:512]))
            var_tok = v.tok()
            s.wait(var_tok)
            s.wait(self.consts_tok)
            s.op(lambda e: e.activation(self.FT[0:1, 0:512],
                                        self.LNROW[0:1, 512:1024], AF.Sqrt,
                                        bias=self.EPSC[0:1, 0:1]))
            sd_tok = s.tok()
            v.wait(sd_tok)
            v.op(lambda e: e.reciprocal(self.LNROW[0:1, 512:1024],
                                        self.FT[0:1, 0:512]))
            v.fence()
            v.op(lambda e: e.tensor_copy(self.AB[0:1, R_HS:R_HS + 512],
                                         self.LNROW[0:1, 0:512]))
            v.op(lambda e: e.tensor_copy(self.AB[0:1, R_HS + 512:R_HS + 1024],
                                         self.LNROW[0:1, 512:1024]))
            brow_tok = v.tok()
            t.wait(brow_tok)
            self.mm(self.PS12[:, 0:512], self.ONESR[0:1, :],
                    self.AB[0:1, R_HS:R_HS + 512], True, True)
            self.mm(self.PS12[:, 512:1024], self.ONESR[0:1, :],
                    self.AB[0:1, R_HS + 512:R_HS + 1024], True, True)
            rep_tok = t.tok()
            v.wait(rep_tok)
            ft8 = self.AB[:, R_HE:R_HE + 4096].bitcast(F32)
            for half in range(2):
                cs = range(half * 4, half * 4 + 4)
                for c in cs:
                    v.op(lambda e, _c=c, _t0=t0, _sl=(c % 4) * 512: e.tensor_tensor(
                        ft8[:, _sl:_sl + 512],
                        self.X[:, _c * TC + _t0:_c * TC + _t0 + 512],
                        self.PS12[:, 0:512], ALU.subtract))
                v.fence()
                for c in cs:
                    v.op(lambda e, _c=c, _t0=t0, _sl=(c % 4) * 512: e.tensor_tensor(
                        self.ab(R_XN + _c * TC + _t0, 512), ft8[:, _sl:_sl + 512],
                        self.PS12[:, 512:1024], ALU.mult))
                if half == 0:
                    v.fence()
            last_apply = v.tok()
        return last_apply

    def router(self, xn_tok, level=99):
        t, v, s = self.t, self.v, self.s
        wr_tok = self.gdma("gwr", self.ab3(R_WR, NCH, NE),
                           self.d_wr.ap().rearrange("(k p) e -> p k e", p=128))
        t.wait(xn_tok)
        t.wait(wr_tok)
        wr = self.ab3(R_WR, NCH, NE)
        for tc in range(16):
            for c in range(NCH):
                self.mm(self.PS4[:, tc * 16:(tc + 1) * 16],
                        self.ab(R_XN + c * TC + tc * 128, 128),
                        wr[:, c, :], c == 0, c == NCH - 1)
        logits_tok = t.tok()
        P, PM, TMP, CW = 0, 256, 512, 768
        NM, SU_, RC, M1, M2 = 1024, 1040, 1056, 1072, 1088
        v.wait(logits_tok)
        v.op(lambda e: e.tensor_copy(self.SM[:, P:P + 256], self.PS4[:, 0:256]))
        ps4_free = v.tok()
        v.wait(ps4_free)
        if level < 2:
            return ps4_free, ps4_free

        def seg(base, tc):
            return self.SM[:, base + tc * 16:base + (tc + 1) * 16]

        def col(base, tc):
            return self.SM[:, base + tc:base + tc + 1]

        for tc in range(16):
            v.op(lambda e, _tc=tc: e.reduce_max(col(NM, _tc), seg(P, _tc),
                                                axis=AX.X, negate=True))
        v.fence()
        for tc in range(16):
            v.op(lambda e, _tc=tc: e.tensor_scalar(seg(P, _tc), seg(P, _tc),
                                                   col(NM, _tc), None, ALU.add))
        sh_tok = v.tok()
        s.wait(sh_tok)
        s.op(lambda e: e.activation(self.SM[:, P:P + 256], self.SM[:, P:P + 256],
                                    AF.Exp))
        exp_tok = s.tok()
        if level < 3:
            return exp_tok, ps4_free
        v.wait(exp_tok)
        for tc in range(16):
            v.op(lambda e, _tc=tc: e.reduce_sum(col(SU_, _tc), seg(P, _tc), axis=AX.X))
        v.fence()
        v.op(lambda e: e.reciprocal(self.SM[:, RC:RC + 16], self.SM[:, SU_:SU_ + 16]))
        v.fence()
        for tc in range(16):
            v.op(lambda e, _tc=tc: e.tensor_scalar(seg(P, _tc), seg(P, _tc),
                                                   col(RC, _tc), None, ALU.mult))
        v.fence()
        for tc in range(16):
            v.op(lambda e, _tc=tc: e.reduce_max(col(M1, _tc), seg(P, _tc), axis=AX.X))
        v.fence()
        for tc in range(16):
            v.op(lambda e, _tc=tc: e.tensor_scalar(seg(PM, _tc), seg(P, _tc),
                                                   col(M1, _tc), None, ALU.is_ge))
        v.fence()
        for tc in range(16):
            v.op(lambda e, _tc=tc: e.tensor_scalar(seg(TMP, _tc), seg(PM, _tc),
                                                   -1.0, 1.0, ALU.mult, ALU.add))
        v.fence()
        for tc in range(16):
            v.op(lambda e, _tc=tc: e.tensor_mul(seg(TMP, _tc), seg(TMP, _tc),
                                                seg(P, _tc)))
        v.fence()
        for tc in range(16):
            v.op(lambda e, _tc=tc: e.reduce_max(col(M2, _tc), seg(TMP, _tc), axis=AX.X))
        v.fence()
        for tc in range(16):
            v.op(lambda e, _tc=tc: e.tensor_scalar(seg(TMP, _tc), seg(TMP, _tc),
                                                   col(M2, _tc), None, ALU.is_ge))
        v.fence()
        for tc in range(16):
            v.op(lambda e, _tc=tc: e.tensor_add(seg(TMP, _tc), seg(TMP, _tc),
                                                seg(PM, _tc)))
        v.fence()
        for tc in range(16):
            v.op(lambda e, _tc=tc: e.scalar_tensor_tensor(
                seg(CW, _tc), seg(TMP, _tc), 0.5, seg(P, _tc), ALU.mult, ALU.mult))
        v.fence()
        v.op(lambda e: e.tensor_copy(self.ab(R_CWB, 256), self.SM[:, CW:CW + 256]))
        cwb_tok = v.tok()
        if level < 4:
            return cwb_tok, ps4_free
        t.wait(cwb_tok)
        pt_free = None
        cwt_tok = None
        for tc in range(16):
            t.wait(pt_free)
            t.op(lambda e, _tc=tc: e.transpose(
                self.PTF[0:16, 0:64].bitcast(BF16),
                self.ab(R_CWB + _tc * 16, 16), self.IDN[:, :]))
            tt = t.tok()
            v.wait(tt)
            v.op(lambda e, _tc=tc: e.tensor_copy(
                self.ab(R_CWT + _tc * 128, 128, 0, 16),
                self.PTF[0:16, 0:64].bitcast(BF16)))
            pt_free = v.tok()
            cwt_tok = pt_free
        return cwt_tok, ps4_free

    def ffn_call(self, xn_tok, n_experts=NE, with_shared=True):
        t, v, s, g, y = self.t, self.v, self.s, self.g, self.y
        self.gdma("gsh", self.ab3(R_SG, NCH, FS),
                  self.d_sg.ap().rearrange("(k p) f -> p k f", p=128))
        sg_tok = self.gdma("gsh", self.ab3(R_SU, NCH, FS),
                           self.d_su.ap().rearrange("(k p) f -> p k f", p=128))
        sd_tok = self.gdma("gsd", self.ab3(R_SD, NCH, D),
                           self.d_sd.ap().rearrange("(k p) f -> p k f", p=128))
        cwt_tok, ps4_free0 = self.router(xn_tok)

        pool = Rot(self.pool_slices(with_ps4=False, with_ps3=True))
        t.wait(xn_tok)
        t.wait(sg_tok)
        sgv = self.ab3(R_SG, NCH, FS)
        suv = self.ab3(R_SU, NCH, FS)
        sdv = self.ab3(R_SD, NCH, D)
        last_gu_tok = None

        hs_slots = [R_HS, R_HE]
        slot_free = [None, None]
        pend = None

        def emit_shared_d(p):
            tb_, slot_, h_tok_ = p
            t0_ = tb_ * 512
            hv = self.ab3(hs_slots[slot_], NCH, 512)
            t.wait(sd_tok)
            t.wait(h_tok_)
            last_d = None
            for dc in range(NCH):
                sld = pool.get(t)
                for c in range(NCH):
                    self.mm(slap(sld), sdv[:, c, dc * 128:(dc + 1) * 128],
                            hv[:, c, :], c == 0, c == NCH - 1)
                dtok = t.tok()
                last_d = dtok
                v.wait(dtok)
                v.op(lambda e, _dc=dc, _t0=t0_, _sl=sld: e.tensor_tensor(
                    self.X[:, _dc * TC + _t0:_dc * TC + _t0 + 512],
                    self.X[:, _dc * TC + _t0:_dc * TC + _t0 + 512],
                    slap(_sl), ALU.add))
                pool.put(sld, v.tok())
            slot_free[slot_] = last_d

        for tb in range(4 if with_shared else 0):
            t0 = tb * 512
            slot = tb % 2
            base = hs_slots[slot]
            t.wait(slot_free[slot])
            h_tok = None
            for fc in range(NCH):
                slg = pool.get(t)
                for c in range(NCH):
                    self.mm(slap(slg), sgv[:, c, fc * 128:(fc + 1) * 128],
                            self.ab(R_XN + c * TC + t0, 512), c == 0, c == NCH - 1)
                gtok = t.tok()
                slu = pool.get(t)
                for c in range(NCH):
                    self.mm(slap(slu), suv[:, c, fc * 128:(fc + 1) * 128],
                            self.ab(R_XN + c * TC + t0, 512), c == 0, c == NCH - 1)
                utok = t.tok()
                last_gu_tok = utok
                s.wait(gtok)
                s.op(lambda e, _fc=fc, _sl=slg, _b=base: e.activation(
                    self.ab(_b + _fc * 512, 512), slap(_sl), AF.Silu))
                stok = s.tok()
                pool.put(slg, stok)
                v.wait(stok)
                v.wait(utok)
                v.op(lambda e, _fc=fc, _sl=slu, _b=base: e.scalar_tensor_tensor(
                    self.ab(_b + _fc * 512, 512), self.ab(_b + _fc * 512, 512),
                    0.5, slap(_sl), ALU.mult, ALU.mult))
                h_tok = v.tok()
                pool.put(slu, h_tok)
            if pend is not None:
                emit_shared_d(pend)
            pend = (tb, slot, h_tok)
        if pend is not None:
            emit_shared_d(pend)

        he_slots = [R_HE, R_HE + 2048, R_HS, R_HS + 2048]
        he_free = [None, None, None, None]
        crep_ps = [self.PS4, self.PTF]
        crep_free = [ps4_free0, cwt_tok]
        exp_w_free = [None, None]
        w_toks = [None, None]
        crow_tok = [None, None]
        crow_free = [None, None]
        regions = [R_EXPA, R_EXPB]
        epend = None

        def load_expert(e_i):
            par = e_i % 2
            reg = regions[par]
            grp = "gwA" if par == 0 else "gwB"
            g.wait(exp_w_free[par])
            if e_i == 1:
                g.wait(last_gu_tok)
            self.gdma(grp, self.ab3(reg, NCH, FE),
                      self.d_wg[e_i].rearrange("(k p) f -> p k f", p=128))
            self.gdma(grp, self.ab3(reg + 4096, NCH, FE),
                      self.d_wu[e_i].rearrange("(k p) f -> p k f", p=128))
            w_toks[par] = self.gdma(grp, self.ab3(reg + 8192, 4, D),
                                    self.d_wd[e_i].rearrange("(k p) f -> p k f", p=128))
            y.wait(cwt_tok)
            y.wait(crow_free[par])
            crow_tok[par] = self.ydma("ycrA" if par == 0 else "ycrB",
                                      self.CROW[0:1, par * TC:(par + 1) * TC],
                                      self.ab(R_CWT, TC, e_i, e_i + 1))

        def emit_exp_d(p):
            slot_, hc_tok_, wdv_, t0_, par_, is_last_tb_ = p
            hv = self.ab3(he_slots[slot_], 4, 512)
            t.wait(hc_tok_)
            last_d = None
            for dc in range(NCH):
                sld = pool.get(t)
                for c in range(4):
                    self.mm(slap(sld), wdv_[:, c, dc * 128:(dc + 1) * 128],
                            hv[:, c, :], c == 0, c == 3)
                dtok = t.tok()
                last_d = dtok
                v.wait(dtok)
                v.op(lambda e, _dc=dc, _t0=t0_, _sl=sld: e.tensor_tensor(
                    self.X[:, _dc * TC + _t0:_dc * TC + _t0 + 512],
                    self.X[:, _dc * TC + _t0:_dc * TC + _t0 + 512],
                    slap(_sl), ALU.add))
                pool.put(sld, v.tok())
            he_free[slot_] = last_d
            if is_last_tb_:
                exp_w_free[par_] = last_d

        if n_experts > 0:
            load_expert(0)
        for e_i in range(n_experts):
            par = e_i % 2
            reg = regions[par]
            wgv = self.ab3(reg, NCH, FE)
            wuv = self.ab3(reg + 4096, NCH, FE)
            wdv = self.ab3(reg + 8192, 4, D)
            t.wait(w_toks[par])
            t.wait(crow_tok[par])
            last_crep = None
            for tb in range(4):
                t0 = tb * 512
                gi = e_i * 4 + tb
                slot = gi % 4
                cs = gi % 2
                cps = crep_ps[cs]
                t.wait(crep_free[cs])
                self.mm(cps[:, 0:512], self.ONESR[0:1, :],
                        self.CROW[0:1, par * TC + t0:par * TC + t0 + 512], True, True)
                crep_tok = t.tok()
                last_crep = crep_tok
                base = he_slots[slot]
                t.wait(he_free[slot])
                for fc in range(4):
                    slg = pool.get(t)
                    for c in range(NCH):
                        self.mm(slap(slg), wgv[:, c, fc * 128:(fc + 1) * 128],
                                self.ab(R_XN + c * TC + t0, 512), c == 0, c == NCH - 1)
                    gtok = t.tok()
                    slu = pool.get(t)
                    for c in range(NCH):
                        self.mm(slap(slu), wuv[:, c, fc * 128:(fc + 1) * 128],
                                self.ab(R_XN + c * TC + t0, 512), c == 0, c == NCH - 1)
                    utok = t.tok()
                    s.wait(gtok)
                    s.op(lambda e, _fc=fc, _sl=slg, _b=base: e.activation(
                        self.ab(_b + _fc * 512, 512), slap(_sl), AF.Silu))
                    stok = s.tok()
                    pool.put(slg, stok)
                    v.wait(stok)
                    v.wait(utok)
                    v.op(lambda e, _fc=fc, _sl=slu, _b=base: e.tensor_tensor(
                        self.ab(_b + _fc * 512, 512), self.ab(_b + _fc * 512, 512),
                        slap(_sl), ALU.mult))
                    pool.put(slu, v.tok())
                v.wait(crep_tok)
                v.fence()
                for fc in range(4):
                    v.op(lambda e, _fc=fc, _b=base, _c=cps: e.tensor_tensor(
                        self.ab(_b + _fc * 512, 512), self.ab(_b + _fc * 512, 512),
                        _c[:, 0:512], ALU.mult))
                hc_tok = v.tok()
                crep_free[cs] = hc_tok
                epend_new = (slot, hc_tok, wdv, t0, par, tb == 3)
                if epend is not None:
                    emit_exp_d(epend)
                epend = epend_new
                if tb == 0 and e_i + 1 < n_experts:
                    load_expert(e_i + 1)
            crow_free[par] = last_crep
        if epend is not None:
            emit_exp_d(epend)
        pool.drain(t)
        return self.v.mark()

    def gather(self, xn_tok, t_done):
        g, y, v = self.g, self.y, self.v
        y.wait(xn_tok)
        bounce_tok = self.ydma("ybc", self.d_ccin.ap().rearrange("(k p) t -> p k t", p=128),
                               self.ab3(R_XN, NCH, TC))
        g.wait(bounce_tok)
        self.cc_cnt += 1
        cc = self.cc
        n = self.cc_cnt
        g.op(lambda e: e.collective_compute(
            "AllGather", ALU.bypass,
            replica_groups=[[0, 1], [2, 3], [4, 5], [6, 7]],
            ins=[self.d_ccin.ap().opt()],
            outs=[self.d_ccout.ap().opt()]).then_inc(cc, 1))
        g.mark_dma()
        cc_tok = (cc, n)
        v.wait(xn_tok)
        for c in range(NCH):
            v.op(lambda e, _c=c: e.tensor_copy(
                self.ab(R_XNF + _c * SEQ, TC), self.ab(R_XN + _c * TC, TC)))
        own_tok = v.tok()
        g.wait(t_done)
        g.wait(cc_tok)
        y.wait(cc_tok)
        xnf = self.ab3(R_XNF, NCH, SEQ)
        scr = self.ab3(R_QKVW, NCH, TC)
        tok_a = self.gdma("gpr", xnf[:, :, TC:SEQ],
                          self.d_ccout[0].rearrange("(k p) t -> p k t", p=128))
        tok_b = self.ydma("ybc", scr,
                          self.d_ccout[1].rearrange("(k p) t -> p k t", p=128))
        v.wait(tok_a)
        v.wait(tok_b)
        v.wait(self.po_tok)
        for c in range(NCH):
            v.op(lambda e, _c=c: e.tensor_scalar(
                self.ab(R_QKVW + _c * TC, TC), self.ab(R_QKVW + _c * TC, TC),
                self.POS[:, 0:1], None, ALU.mult))
        v.fence()
        for c in range(NCH):
            v.op(lambda e, _c=c: e.scalar_tensor_tensor(
                self.ab(R_XNF + _c * SEQ + TC, TC),
                self.ab(R_XNF + _c * SEQ + TC, TC),
                self.POS[:, 1:2],
                self.ab(R_QKVW + _c * TC, TC),
                ALU.mult, ALU.add))
        pair_tok = v.tok()
        return own_tok, pair_tok

    def attention(self, own_tok, pair_tok, guards=()):
        t, v, s, g, y = self.t, self.v, self.s, self.g, self.y
        for gk in guards:
            t.wait(gk)
        xnf = self.ab3(R_XNF, NCH, SEQ)
        pool = Rot(self.pool_slices(with_ps4=True, with_ps3=True))
        qkvw_free = [None, None]
        projw_free = [None, None]
        qkv_toks = [None, None]
        proj_toks = [None, None]
        ao_free = [None, None]
        kbuf_free = None
        vslab_free = None
        q_free = None
        vt_free = None
        a_free = [None, None, None, None]
        scores_ps_free = [None, None]
        av_ps_free = None
        qlv_free = None

        def load_pair_w(p):
            par = p % 2
            grp = "gqA" if par == 0 else "gqB"
            if p < 2:
                g.wait(pair_tok)
            g.wait(qkvw_free[par])
            g.wait(projw_free[par])
            base = R_QKVW + par * 3072
            w3 = self.ab3(base, NCH, 384)
            for j, c0 in enumerate((128 * p, D + 128 * p, 2 * D + 128 * p)):
                self.gdma(grp, w3[:, :, j * 128:(j + 1) * 128],
                          self.d_wqkv[:, c0:c0 + 128].rearrange(
                              "(k p) c -> p k c", p=128))
            tokw = self.gdma(grp, self.ab(R_PROJW + par * D, D),
                             self.d_wproj[128 * p:128 * (p + 1), :])
            qkv_toks[par] = tokw
            proj_toks[par] = tokw

        g.wait(own_tok)
        g.wait(pair_tok)
        g.op(lambda e: e.memset(self.ab(R_VS, 2 * SLAB), 1.0))
        vs_init = g.tok()
        vslab_free = vs_init
        load_pair_w(0)
        for p in range(8):
            par = p % 2
            if p + 1 < 8:
                load_pair_w(p + 1)
            w3 = self.ab3(R_QKVW + par * 3072, NCH, 384)
            t.wait(qkv_toks[par])
            s.wait(self.bias_tok)
            t.wait(kbuf_free)
            for tbl in range(8):
                t.wait(own_tok if tbl < 4 else pair_tok)
                sl = pool.get(t)
                for c in range(NCH):
                    self.mm(slap(sl), w3[:, c, 128:256],
                            xnf[:, c, tbl * 512:(tbl + 1) * 512], c == 0, c == NCH - 1)
                ktok = t.tok()
                s.wait(ktok)
                s.op(lambda e, _tbl=tbl, _sl=sl, _p=p: e.activation(
                    self.ab(R_K + _tbl * 512, 512), slap(_sl), AF.Identity,
                    bias=self.BIAS[:, 8 + _p:9 + _p]))
                pool.put(sl, s.tok())
            kb_tok = self.s.mark()
            t.wait(vt_free)
            for tbl in range(8):
                t.wait(own_tok if tbl < 4 else pair_tok)
                sl = pool.get(t)
                for c in range(NCH):
                    self.mm(slap(sl), w3[:, c, 256:384],
                            xnf[:, c, tbl * 512:(tbl + 1) * 512], c == 0, c == NCH - 1)
                vtok = t.tok()
                s.wait(vtok)
                s.op(lambda e, _tbl=tbl, _sl=sl, _p=p: e.activation(
                    self.ab(R_VT + _tbl * 512, 512), slap(_sl), AF.Identity,
                    bias=self.BIAS[:, 16 + _p:17 + _p]))
                pool.put(sl, s.tok())
            vt_tok = self.s.mark()
            t.wait(q_free)
            t.wait(own_tok)
            for tbl in range(4):
                sl = pool.get(t)
                for c in range(NCH):
                    self.mm(slap(sl), w3[:, c, 0:128],
                            xnf[:, c, tbl * 512:(tbl + 1) * 512], c == 0, c == NCH - 1)
                qtok = t.tok()
                s.wait(qtok)
                s.op(lambda e, _tbl=tbl, _sl=sl, _p=p: e.activation(
                    self.ab(R_Q + _tbl * 512, 512), slap(_sl), AF.Identity,
                    bias=self.BIAS[:, _p:_p + 1]))
                pool.put(sl, s.tok())
            q_tok = self.s.mark()
            t.wait(q_tok)
            t.wait(self.lora_tok)
            t.wait(qlv_free)
            for hh in range(2):
                b0 = hh * 64
                for tbl in range(4):
                    sl = pool.get(t)
                    self.mm(slap(sl, 0, 16), self.AQ[b0:b0 + 64, :],
                            self.ab(R_Q + tbl * 512, 512, b0, b0 + 64), True, True,
                            tp=(b0, 0))
                    qlt = t.tok()
                    s.wait(qlt)
                    s.op(lambda e, _tbl=tbl, _sl=sl, _hh=hh: e.activation(
                        self.ab(R_QLV + _hh * TC + _tbl * 512, 512, 0, 16),
                        slap(_sl, 0, 16), AF.Identity, scale=2.0))
                    pool.put(sl, s.tok())
            ql_tok = self.s.mark()
            t.wait(ql_tok)
            for hh in range(2):
                b0 = hh * 64
                for tbl in range(4):
                    sl = pool.get(t)
                    self.mm(slap(sl, b0, b0 + 64), self.BQ[:, :],
                            self.ab(R_QLV + hh * TC + tbl * 512, 512, 0, 16),
                            True, True, tp=(0, b0))
                    ct = t.tok()
                    v.wait(ct)
                    v.op(lambda e, _tbl=tbl, _sl=sl, _b0=b0: e.tensor_tensor(
                        self.ab(R_Q + _tbl * 512, 512, _b0, _b0 + 64),
                        self.ab(R_Q + _tbl * 512, 512, _b0, _b0 + 64),
                        slap(_sl, _b0, _b0 + 64), ALU.add))
                    pool.put(sl, v.tok())
            qfinal_tok = self.v.mark()
            t.wait(vt_tok)
            for hh in range(2):
                b0 = hh * 64
                for tbl in range(8):
                    sl = pool.get(t)
                    self.mm(slap(sl, 0, 16), self.AVm[b0:b0 + 64, :],
                            self.ab(R_VT + tbl * 512, 512, b0, b0 + 64), True, True,
                            tp=(b0, 0))
                    vlt = t.tok()
                    s.wait(vlt)
                    s.op(lambda e, _tbl=tbl, _sl=sl: e.activation(
                        self.ab(R_QLV + _tbl * 512, 512, 0, 16),
                        slap(_sl, 0, 16), AF.Identity, scale=2.0))
                    pool.put(sl, s.tok())
                vl_tok = self.s.mark()
                t.wait(vl_tok)
                for tbl in range(8):
                    sl = pool.get(t)
                    self.mm(slap(sl, b0, b0 + 64), self.BV[:, :],
                            self.ab(R_QLV + tbl * 512, 512, 0, 16),
                            True, True, tp=(0, b0))
                    ct = t.tok()
                    v.wait(ct)
                    v.op(lambda e, _tbl=tbl, _sl=sl, _b0=b0: e.tensor_tensor(
                        self.ab(R_VT + _tbl * 512, 512, _b0, _b0 + 64),
                        self.ab(R_VT + _tbl * 512, 512, _b0, _b0 + 64),
                        slap(_sl, _b0, _b0 + 64), ALU.add))
                    pool.put(sl, v.tok())
            vfinal_tok = self.v.mark()
            qlv_free = vfinal_tok
            pool.drain(t)
            t.wait(vfinal_tok)
            t.wait(vslab_free)
            v.wait(vslab_free)
            tsl = [self.PS12[:, 0:512], self.PS12[:, 512:1024]]
            pt_free = [None, None]
            for tc in range(32):
                sl_i = tc % 2
                t.wait(pt_free[sl_i])
                t.op(lambda e, _tc=tc, _s=tsl[sl_i]: e.transpose(
                    _s[:, 0:64].bitcast(BF16),
                    self.ab(R_VT + _tc * 128, 128), self.IDN[:, :]))
                trt = t.tok()
                v.wait(trt)
                v.op(lambda e, _tc=tc, _s=tsl[sl_i]: e.tensor_copy(
                    self.ab(R_VS + _tc * 65, 64),
                    _s[:, 0:32].bitcast(BF16)))
                v.op(lambda e, _tc=tc, _s=tsl[sl_i]: e.tensor_copy(
                    self.ab(R_VS + SLAB + _tc * 65, 64),
                    _s[:, 32:64].bitcast(BF16)))
                pt_free[sl_i] = v.tok()
            slab_tok = self.v.mark()
            vt_free = self.t.mark()
            t.wait(slab_tok)
            t.wait(qfinal_tok)
            ao_base = R_AO + par * TC
            v.wait(ao_free[par])
            avs = [(self.PS3, 0), (self.PS3, 512), (self.PS4, 0), (self.PTF, 0)]
            norm_tok = None
            sps_free = [None, None]
            a_free2 = [[None, None], [None, None]]
            for qp in range(2):
                q0 = qp * 1024
                for kc in range(32):
                    apar = kc % 2
                    exp_tk = [None, None]
                    for hh in range(2):
                        b0 = hh * 64
                        c0 = hh * 1024
                        t.wait(sps_free[hh])
                        t.wait(a_free2[hh][apar])
                        for i in range(2):
                            self.mm(self.PS12[:, c0 + i * 512:c0 + i * 512 + 512],
                                    self.ab(R_K + kc * 128, 128, b0, b0 + 64),
                                    self.ab(R_Q + q0 + i * 512, 512, b0, b0 + 64),
                                    True, True, tp=(b0, 0))
                        sc_tok = t.tok()
                        s.wait(sc_tok)
                        s.op(lambda e, _hh=hh, _ap=apar, _c0=c0: e.activation(
                            self.ab(R_A + (_hh * 2 + _ap) * 1024, 1024),
                            self.PS12[:, _c0:_c0 + 1024],
                            AF.Exp, scale=ATTN_SCALE))
                        sps_free[hh] = s.tok()
                        exp_tk[hh] = sps_free[hh]
                    for hh in range(2):
                        slot = R_A + (hh * 2 + apar) * 1024
                        t.wait(exp_tk[hh])
                        if kc == 0:
                            t.wait(norm_tok)
                        for i in range(2):
                            av = avs[hh * 2 + i]
                            self.mm(av[0][0:65, av[1]:av[1] + 512],
                                    self.ab(R_VS + hh * SLAB + kc * 65, 65),
                                    self.ab(slot + i * 512, 512),
                                    kc == 0, kc == 31)
                        a_free2[hh][apar] = t.tok()
                av_tok = self.t.mark()
                f32a = self.AB[:, R_A:R_A + 2048].bitcast(F32)
                f32b = self.AB[:, R_A + 2048:R_A + 4096].bitcast(F32)
                v.wait(av_tok)
                v.op(lambda e: e.tensor_copy(f32a[0:1, 0:512], self.PS3[64:65, 0:512]))
                v.op(lambda e: e.tensor_copy(f32a[0:1, 512:1024],
                                             self.PS3[64:65, 512:1024]))
                v.op(lambda e: e.tensor_copy(f32a[32:33, 0:512], self.PS4[64:65, 0:512]))
                v.op(lambda e: e.tensor_copy(f32a[32:33, 512:1024],
                                             self.PTF[64:65, 0:512]))
                v.fence()
                v.op(lambda e: e.reciprocal(f32b[0:1, 0:1024], f32a[0:1, 0:1024]))
                v.op(lambda e: e.reciprocal(f32b[32:33, 0:1024], f32a[32:33, 0:1024]))
                v.fence()
                rb = R_QLV + TC
                v.op(lambda e, _rb=rb: e.tensor_copy(self.AB[0:1, _rb:_rb + 1024],
                                                     f32b[0:1, 0:1024]))
                v.op(lambda e, _rb=rb: e.tensor_copy(self.AB[32:33, _rb:_rb + 1024],
                                                     f32b[32:33, 0:1024]))
                rc_tok = v.tok()
                scr = R_A + 4096 - 2048
                scr = R_QLV
                v.wait(av_tok)
                for j, av in enumerate(avs):
                    v.op(lambda e, _j=j, _av=av: e.tensor_copy(
                        self.ab(scr + _j * 512, 512, 0, 64),
                        _av[0][0:64, _av[1]:_av[1] + 512]))
                av_evac_tok = v.tok()
                t.wait(rc_tok)
                t.wait(av_evac_tok)
                self.mm(self.PS3[0:64, 0:512], self.ONESR[0:1, 0:64],
                        self.AB[0:1, rb:rb + 512], True, True, tp=(0, 0))
                self.mm(self.PS3[0:64, 512:1024], self.ONESR[0:1, 0:64],
                        self.AB[0:1, rb + 512:rb + 1024], True, True, tp=(0, 0))
                self.mm(self.PS4[64:128, 0:512], self.ONESR[32:33, 0:64],
                        self.AB[32:33, rb:rb + 512], True, True, tp=(32, 64))
                self.mm(self.PTF[64:128, 0:512], self.ONESR[32:33, 0:64],
                        self.AB[32:33, rb + 512:rb + 1024], True, True, tp=(32, 64))
                rep_tok = t.tok()
                norm_ps_free = rep_tok
                v.wait(rep_tok)
                for j in range(4):
                    hh, qbi = j // 2, j % 2
                    if hh == 0:
                        rep_ap = self.PS3[0:64, qbi * 512:qbi * 512 + 512]
                    elif qbi == 0:
                        rep_ap = self.PS4[64:128, 0:512]
                    else:
                        rep_ap = self.PTF[64:128, 0:512]
                    p0b = hh * 64
                    v.op(lambda e, _j=j, _qbi=qbi, _rep=rep_ap, _p0=p0b,
                         _q0=q0, _ao=ao_base: e.tensor_tensor(
                        self.ab(_ao + _q0 + _qbi * 512, 512, _p0, _p0 + 64),
                        self.ab(scr + _j * 512, 512, 0, 64),
                        _rep, ALU.mult))
                norm_tok = v.tok()
            ao_tok = self.v.mark()
            kbuf_free = self.t.mark()
            vslab_free = kbuf_free
            q_free = kbuf_free
            qkvw_free[par] = kbuf_free
            t.wait(proj_toks[par])
            t.wait(ao_tok)
            for dc in range(NCH):
                for tbl in range(4):
                    sl = pool.get(t)
                    self.mm(slap(sl), self.ab(R_PROJW + par * D + dc * 128, 128),
                            self.ab(ao_base + tbl * 512, 512), True, True)
                    ptok = t.tok()
                    v.wait(ptok)
                    v.op(lambda e, _dc=dc, _tbl=tbl, _sl=sl: e.tensor_tensor(
                        self.X[:, _dc * TC + _tbl * 512:_dc * TC + _tbl * 512 + 512],
                        self.X[:, _dc * TC + _tbl * 512:_dc * TC + _tbl * 512 + 512],
                        slap(_sl), ALU.add))
                    pool.put(sl, v.tok())
            projw_free[par] = self.t.mark()
            ao_free[par] = projw_free[par]
        xadd_tok = self.v.mark()
        s.wait(xadd_tok)
        s.wait(self.biasp_tok)
        for dc in range(NCH):
            s.op(lambda e, _dc=dc: e.add(self.X[:, _dc * TC:(_dc + 1) * TC],
                                         self.X[:, _dc * TC:(_dc + 1) * TC],
                                         self.BIAS[:, 24 + _dc:25 + _dc]))
        return self.s.mark()

    def dump_bf16(self, name, off, k, f):
        d = self.add_dbg(name, [k * 128, f], BF16)
        tk = self.ydma("yout", d.ap().rearrange("(k p) t -> p k t", p=128),
                       self.ab3(off, k, f))
        self.y.wait(tk)

    def dump_X(self, name):
        d = self.add_dbg(name, [D, TC], F32)
        tk = self.ydma("yout", d.ap().rearrange("(k p) t -> p k t", p=128),
                       self.X[:, :].rearrange("p (k t) -> p k t", k=NCH))
        self.y.wait(tk)

    def dump_sm(self, name, c0, c1):
        d = self.add_dbg(name, [128, c1 - c0], F32)
        tk = self.ydma("yout", d[:, :], self.SM[:, c0:c1])
        self.y.wait(tk)

    def build(self):
        ds = self.debug_stage
        self.init_consts()
        xn_tok = self.layernorm(self.x_tok)
        if ds == "ln1":
            self.barrier()
            self.dump_bf16("xn", R_XN, NCH, TC)
            return self.finish()
        if ds and ds.startswith("rl"):
            self.router(xn_tok, level=int(ds[2:]))
            self.barrier()
            self.dump_sm("probs", 0, 256)
            return self.finish()
        if ds == "router":
            self.router(xn_tok)
            self.barrier()
            self.dump_sm("probs", 0, 256)
            self.dump_sm("cw", 768, 1024)
            d = self.add_dbg("cwt", [16, TC], BF16)
            tk = self.ydma("yout", d[:, :], self.ab(R_CWT, TC, 0, 16))
            self.y.wait(tk)
            return self.finish()
        if ds and ds.startswith("ffn1"):
            ne = NE
            wsh = True
            if ds == "ffn1s":
                ne = 0
            elif ds.startswith("ffn1e"):
                ne = int(ds[5:])
                wsh = False
            self.ffn_call(xn_tok, n_experts=ne, with_shared=wsh)
            self.barrier()
            self.dump_X("x1")
            return self.finish()
        self.ffn_call(xn_tok)
        self.barrier()
        xn2_tok = self.layernorm(self.v.mark())
        own_tok, pair_tok = self.gather(xn2_tok, self.t.mark())
        if ds == "gather":
            self.barrier()
            self.dump_bf16("xnf", R_XNF, NCH, SEQ)
            return self.finish()
        self.attention(own_tok, pair_tok,
                       guards=(self.v.mark(), self.s.mark()))
        self.barrier()
        if ds == "attn":
            self.dump_X("x2")
            return self.finish()
        xn3_tok = self.layernorm(self.s.mark())
        self.ffn_call(xn3_tok)
        self.barrier()
        out_tok = self.ydma("yout", self.d_o.ap().rearrange("(k p) t -> p k t", p=128),
                            self.X[:, :].rearrange("p (k t) -> p k t", k=NCH))
        self.y.wait(out_tok)
        return self.finish()

    def finish(self):
        nc = self.nc
        with nc.Block() as block:
            @block.tensor
            def _(e):
                for f in self.t.ops:
                    f(e)

            @block.vector
            def _(e):
                for f in self.v.ops:
                    f(e)

            @block.scalar
            def _(e):
                for f in self.s.ops:
                    f(e)

            @block.gpsimd
            def _(e):
                for f in self.g.ops:
                    f(e)

            @block.sync
            def _(e):
                for f in self.y.ops:
                    f(e)
        self.stack.close()
        return nc


def build_nc(debug_stage=None):
    return Builder(debug_stage).build()


def make_in_maps(inputs):
    x = np.asarray(inputs["L0_attn_x"], dtype=np.float32).reshape(16384, D)
    cf = np.ascontiguousarray
    common = {
        "wqkv": cf(np.asarray(inputs["L0_attn_w_qkv"], np.float32)),
        "bqkv": cf(np.asarray(inputs["L0_attn_b_qkv"], np.float32)),
        "aq": cf(np.asarray(inputs["L0_attn_a_q"], np.float32)),
        "bq": cf(np.asarray(inputs["L0_attn_b_q"], np.float32)),
        "av": cf(np.asarray(inputs["L0_attn_a_v"], np.float32)),
        "bv": cf(np.asarray(inputs["L0_attn_b_v"], np.float32)),
        "wproj": cf(np.asarray(inputs["L0_attn_w_proj"], np.float32)),
        "bproj": cf(np.asarray(inputs["L0_attn_b_proj"], np.float32)),
        "wr": cf(np.asarray(inputs["L0_ffn_Wr"], np.float32)),
        "wg": cf(np.asarray(inputs["L0_ffn_Wg"], np.float32)),
        "wu": cf(np.asarray(inputs["L0_ffn_Wu"], np.float32)),
        "wd": cf(np.asarray(inputs["L0_ffn_Wd"], np.float32)),
        "sg": cf(np.asarray(inputs["L0_ffn_Sg"], np.float32)),
        "su": cf(np.asarray(inputs["L0_ffn_Su"], np.float32)),
        "sd": cf(np.asarray(inputs["L0_ffn_Sd"], np.float32)),
    }
    maps = []
    for c in range(8):
        m = dict(common)
        m["xT"] = np.ascontiguousarray(x[c * TC:(c + 1) * TC].T)
        mval = 1.0 - (c % 2)
        m["po"] = np.ascontiguousarray(
            np.tile(np.array([[mval, 1.0 - mval]], np.float32), (128, 1)))
        maps.append(m)
    return maps


_NC = None


def kernel(**inputs):
    global _NC
    if _NC is None:
        _NC = build_nc()
    maps = make_in_maps(inputs)
    res = run_bass_kernel_spmd(_NC, maps, core_ids=list(range(8)))
    out = np.empty((16384, D), np.float32)
    for c in range(8):
        out[c * TC:(c + 1) * TC] = res.results[c]["o"].T
    return out.reshape(4, SEQ, D)
